# version 33
# baseline (speedup 1.0000x reference)
import numpy as np
import ml_dtypes
from contextlib import ExitStack

import concourse.bass as bass
import concourse.mybir as mybir
from concourse.bass_utils import run_bass_kernel_spmd

F32 = mybir.dt.float32
BF16 = mybir.dt.bfloat16
AX = mybir.AxisListType.X
OP = mybir.AluOpType
ACT = mybir.ActivationFunctionType
NPBF = ml_dtypes.bfloat16

NTOK = 8192
D = 2048
NH = 16
HD = 128
NI = 8192
NCORE = 8
TOWN = NTOK // NCORE
HOWN = NH // NCORE
FOWN = HOWN * HD
QB = 512
NQB = NTOK // QB
TW = TOWN // 2
VCW = 2 * (HD + 1)
MAGIC = float(np.float32(3 * (2 ** 22)))
EPS_LN = 1e-5
EPS_ATT = 1e-6
EPS_Q = 1e-5
C_FWHT = float(np.float32(1.0 / np.sqrt(np.float32(8192.0))))

ENGS = ("sync", "vector", "scalar", "tensor", "gpsimd")


def _hadamard(m):
    h = np.array([[1.0]], np.float32)
    while h.shape[0] < m:
        h = np.block([[h, h], [h, -h]])
    return h.astype(np.float32)


def _host_consts():
    ident = np.eye(128, dtype=np.float32)
    theta = (10000.0 ** (-np.arange(64, dtype=np.float32) * 2.0 / 128.0)).astype(np.float32)
    pos = np.arange(NTOK, dtype=np.float32)
    ang = pos[None, :] * theta[:, None]
    masks = np.zeros((128, 4, QB), np.float32)
    for kc in range(4):
        kt = kc * 128 + np.arange(128)[:, None]
        qt = np.arange(QB)[None, :]
        masks[:, kc, :] = (kt <= qt).astype(np.float32)
    return dict(H128b=_hadamard(128).astype(NPBF),
                identb=ident.astype(NPBF), identf=ident,
                cosT=np.cos(ang).astype(np.float32),
                sinT=np.sin(ang).astype(np.float32),
                maskb=masks.reshape(128, 4 * QB).astype(NPBF),
                onesf=np.ones((128, 128), np.float32))


class Prog:

    def __init__(self, nc):
        self.nc = nc
        self.q = {e: [] for e in ENGS}
        self._stack = ExitStack()
        self.sems = {}
        self.cnt = {}

    def sem(self, name):
        if name not in self.sems:
            self.sems[name] = self._stack.enter_context(self.nc.semaphore(name))
            self.cnt[name] = 0
        return self.sems[name]

    def c(self, name):
        return self.cnt.get(name, 0)

    def op(self, eng, fn, sem=None, n=1):
        if sem is not None:
            self.sem(sem)
            self.cnt[sem] += n
        self.q[eng].append(("op", fn, sem, n))
        return self.cnt.get(sem)

    def wait(self, eng, sem, val):
        if val is None or val <= 0:
            return
        self.sem(sem)
        self.q[eng].append(("wait", sem, val))

    def wait_cur(self, eng, sem):
        self.wait(eng, sem, self.c(sem))

    def dma(self, eng, out, in_, sem=None, slow=False):
        if slow:
            return self.op(eng, lambda e, o=out, i=in_: e.dma_start(
                out=o, in_=i, allow_slow_non_contiguous=True), sem=sem, n=16)
        return self.op(eng, lambda e, o=out, i=in_: e.dma_start(out=o, in_=i),
                       sem=sem, n=16)

    def barrier(self, name):
        self.sem(name)
        self.cnt[name] += len(ENGS)
        tgt = self.cnt[name]
        for e in ENGS:
            self.q[e].append(("op", lambda eng: eng.nop(), name, 1))
        for e in ENGS:
            self.q[e].append(("wait", name, tgt))

    def replay(self, blk):
        def mk(eng_name):
            def run(e):
                for item in self.q[eng_name]:
                    if item[0] == "op":
                        _, fn, semname, n = item
                        inst = fn(e)
                        if semname is not None:
                            inst.then_inc(self.sems[semname], n)
                    else:
                        _, semname, val = item
                        e.wait_ge(self.sems[semname], val)
            return run
        blk.sync(mk("sync"))
        blk.vector(mk("vector"))
        blk.scalar(mk("scalar"))
        blk.tensor(mk("tensor"))
        blk.gpsimd(mk("gpsimd"))

    def close(self):
        self._stack.close()


def build_program(debug_taps=()):
    nc = bass.Bass("TRN2", target_bir_lowering=False, debug=False)

    def inp(name, shape, dt=F32):
        return nc.dram_tensor(name, shape, dt, kind="ExternalInput").ap()

    x_own = inp("x_own", [TOWN, D])
    wqT_s = inp("wqT_s", [D, FOWN])
    wkT_s = inp("wkT_s", [D, FOWN])
    wvT_s = inp("wvT_s", [D, FOWN])
    woT_s = inp("woT_s", [FOWN, D])
    gT_s = inp("gT_s", [FOWN, NI])
    uT_s = inp("uT_s", [FOWN, NI])
    dT_s = inp("dT_s", [TOWN, D])
    cosT = inp("cosT", [64, NTOK])
    sinT = inp("sinT", [64, NTOK])
    H128b = inp("H128b", [128, 128], BF16)
    identb = inp("identb", [128, 128], BF16)
    identf = inp("identf", [128, 128])
    onesf = inp("onesf", [128, 128])
    maskb = inp("maskb", [128, 4 * QB], BF16)

    out_own = nc.dram_tensor("out_own", [TOWN, D], F32, kind="ExternalOutput").ap()

    def idram(name, shape, dt=F32, shared=False):
        return nc.dram_tensor(name, shape, dt, kind="Internal",
                              addr_space=("Shared" if shared else "Local")).ap()

    cc_xnT_inA = idram("cc_xnT_inA", [D, TOWN // 2], BF16)
    cc_xnT_inB = idram("cc_xnT_inB", [D, TOWN // 2], BF16)
    cc_absw_in = idram("cc_absw_in", [128, 4], F32)
    cc_xnT_outA = idram("cc_xnT_outA", [NCORE, D, TOWN // 2], BF16, shared=True)
    cc_xnT_outB = idram("cc_xnT_outB", [NCORE, D, TOWN // 2], BF16, shared=True)
    cc_absw_out = idram("cc_absw_out", [NCORE, 128, 4], F32, shared=True)
    cc_tg_in = idram("cc_tg_in", [FOWN, NI], BF16)
    cc_tu_in = idram("cc_tu_in", [FOWN, NI], BF16)
    cc_td_in = idram("cc_td_in", [TOWN, D], BF16)
    cc_tg_out = idram("cc_tg_out", [D, NI], BF16, shared=True)
    cc_tu_out = idram("cc_tu_out", [D, NI], BF16, shared=True)
    cc_td_out = idram("cc_td_out", [NI, D], BF16, shared=True)
    att_partA = idram("att_partA", [NTOK // 2, D], BF16)
    att_partB = idram("att_partB", [NTOK // 2, D], BF16)
    rs_outA = idram("rs_outA", [TOWN // 2, D], BF16)
    rs_outB = idram("rs_outB", [TOWN // 2, D], BF16)
    xtok_park = idram("xtok_park", [TOWN, D], BF16)
    xqatok_park = idram("xqatok_park", [TOWN, D], BF16)
    ypark1 = idram("ypark1", [64, 128, TW], BF16)
    ffn_park = idram("ffn_park", [TOWN, D], F32)
    gam_rt = idram("gam_rt", [2, 512], F32)

    taps = {}
    for name, shape, dt in debug_taps:
        taps[name] = nc.dram_tensor("tap_" + name, shape, dt,
                                    kind="ExternalOutput").ap()

    ctx = ExitStack()
    sb = lambda nm, shp, dt=F32: ctx.enter_context(nc.sbuf_tensor(nm, shp, dt))
    psum = lambda nm, shp, dt=F32: ctx.enter_context(nc.psum_tensor(nm, shp, dt))

    c_H128 = sb("c_H128", [128, 128], BF16)
    c_identb = sb("c_identb", [128, 128], BF16)
    c_identf = sb("c_identf", [128, 128])
    c_onesf = sb("c_onesf", [128, 128])
    c_mask = sb("c_mask", [128, 4 * QB], BF16)
    c_sc = sb("c_sc", [128, 4])
    sc_mean = sb("sc_mean", [128, 4])
    sc_sw = sb("sc_sw", [128, 4])
    invN = sb("invN", [128, 4])
    abs_acc = sb("abs_acc", [128, 32])

    ps_a = psum("ps_a", [128, 1024])
    ps_b = psum("ps_b", [128, 1024])
    ps_o = psum("ps_o", [128, 1024])
    ps_c = psum("ps_c", [128, 1024])
    ps_tb = ps_o[:].bitcast(BF16)
    ps_scA = ps_a[:].bitcast(BF16)
    ps_scB = ps_b[:].bitcast(BF16)

    blk = ctx.enter_context(nc.Block())
    P = Prog(nc)
    ctx.callback(P.close)

    def T(fn, sem=None, n=1):
        return P.op("tensor", fn, sem, n)

    def V(fn, sem=None, n=1):
        return P.op("vector", fn, sem, n)

    def S(fn, sem=None, n=1):
        return P.op("scalar", fn, sem, n)

    def G(fn, sem=None, n=1):
        return P.op("gpsimd", fn, sem, n)

    def Vc(fn):
        P.wait_cur("vector", "vch")
        return V(fn, sem="vch", n=1)

    def Sc(fn):
        P.wait_cur("scalar", "sch")
        return S(fn, sem="sch", n=1)

    MAG_AP = c_sc[:, 0:1]
    ZERO_AP = c_sc[:, 1:2]

    with ExitStack() as ph0:
        sb0 = lambda nm, shp, dt=F32: ph0.enter_context(nc.sbuf_tensor(nm, shp, dt))
        x_t0 = sb0("x_t0", [128, D])
        x_t1 = sb0("x_t1", [128, D])
        ln_sA = sb0("ln_sA", [128, 16])
        ln_sB = sb0("ln_sB", [128, 16])
        xn_t = sb0("xn_t", [128, D])
        xnb_t0 = sb0("xnb_t0", [128, D], BF16)
        xnb_t1 = sb0("xnb_t1", [128, D], BF16)
        xqa_t0p = sb0("xqa_t0p", [128, D], BF16)
        xqa_t1p = sb0("xqa_t1p", [128, D], BF16)
        q_scr0 = sb0("q_scr0", [128, D])
        q_scr1 = sb0("q_scr1", [128, D])
        xns0 = sb0("xns0", [128, 2048], BF16)
        xns1 = sb0("xns1", [128, 2048], BF16)
        wscr = sb0("wscr", [128, 4096])
        wjunk = sb0("wjunk", [128, 4096])
        wjunk2 = sb0("wjunk2", [128, 4096])

        P.dma("sync", c_H128[:], H128b, sem="c_ld")
        P.dma("sync", c_identb[:], identb, sem="c_ld")
        P.dma("sync", c_identf[:], identf, sem="c_ld")
        P.dma("sync", c_onesf[:], onesf, sem="c_ld")
        P.dma("sync", c_mask[:], maskb, sem="c_ld")
        Vc(lambda e: e.memset(c_sc[:, 0:1], MAGIC))
        Vc(lambda e: e.memset(c_sc[:, 1:2], 0.0))
        Vc(lambda e: e.memset(abs_acc[:, 0:16], 0.0))

        abs_pieces = []
        _col = 0
        for (wsrc, ntile, width) in ((gT_s, 2, NI), (uT_s, 2, NI), (dT_s, 8, D)):
            for it in range(ntile):
                for piece in range(max(1, width // 4096)):
                    abs_pieces.append((wsrc, it, piece, min(4096, width), _col))
                    _col += 1
        abs_sc_after = {}

        def emit_abs_piece(idx):
            wsrc, it, piece, pw, c2 = abs_pieces[idx]
            par2 = idx % 2
            wbuf = wscr if par2 == 0 else wjunk
            if idx >= 2:
                P.wait("sync", "abss", abs_sc_after[idx - 2])
            P.dma("sync", wbuf[:, 0:pw],
                  wsrc[it * 128:(it + 1) * 128,
                       piece * 4096:piece * 4096 + pw], sem="abs_ld")
            P.wait("scalar", "abs_ld", P.c("abs_ld"))
            P.wait("scalar", "abss", P.c("abss"))
            S(lambda e, w=pw, cc3=c2, b=wbuf: e.activation(
                wjunk2[:, 0:w], b[:, 0:w], ACT.Abs,
                accum_out=abs_acc[:, cc3:cc3 + 1]), sem="abss", n=1)
            abs_sc_after[idx] = P.c("abss")

        abs_pieces_idx = [0]
        vch_after_x = {}
        sch_after_x = {}
        xpk_after = {}
        for tt in range(8):
            par = tt % 2
            x_t = x_t0 if par == 0 else x_t1
            xnb_t = xnb_t0 if par == 0 else xnb_t1
            xqa_t = xqa_t0p if par == 0 else xqa_t1p
            ln_s = ln_sA if par == 0 else ln_sB
            q_scr = q_scr0 if par == 0 else q_scr1
            if tt >= 2:
                P.wait("sync", "vch", vch_after_x[tt - 2])
                P.wait("sync", "sch", sch_after_x[tt - 2])
            P.dma("sync", x_t[:], x_own[tt * 128:(tt + 1) * 128, :], sem="x_ld")
            P.wait("vector", "x_ld", P.c("x_ld"))
            P.wait("scalar", "x_ld", P.c("x_ld"))
            if tt >= 2:
                P.wait("scalar", "vch", vch_after_x[tt - 2])
            Vc(lambda e, x2=x_t, ls=ln_s, qs=q_scr: e.reduce_sum(ls[:, 0:1], x2[:], axis=AX))
            Sc(lambda e, x2=x_t, ls=ln_s, qs=q_scr: e.activation(qs[:], x2[:], ACT.Square,
                                              accum_out=ls[:, 1:2]))
            P.wait_cur("vector", "sch")
            Vc(lambda e, ls=ln_s, qs=q_scr: e.tensor_scalar(ls[:, 2:3], ls[:, 0:1], 1.0 / D, None, OP.mult))
            Vc(lambda e, ls=ln_s, qs=q_scr: e.tensor_scalar(ls[:, 3:4], ls[:, 1:2], 1.0 / D, None, OP.mult))
            Vc(lambda e, ls=ln_s, qs=q_scr: e.tensor_tensor(ls[:, 4:5], ls[:, 2:3], ls[:, 2:3], OP.mult))
            Vc(lambda e, ls=ln_s, qs=q_scr: e.tensor_tensor(ls[:, 3:4], ls[:, 3:4], ls[:, 4:5], OP.subtract))
            Vc(lambda e, ls=ln_s, qs=q_scr: e.tensor_scalar(ls[:, 3:4], ls[:, 3:4], EPS_LN, None, OP.add))
            P.wait_cur("scalar", "vch")
            Sc(lambda e, ls=ln_s, qs=q_scr: e.activation(ls[:, 5:6], ls[:, 3:4], ACT.Sqrt))
            P.wait_cur("vector", "sch")
            Vc(lambda e, ls=ln_s, qs=q_scr: e.reciprocal(ls[:, 6:7], ls[:, 5:6]))
            Vc(lambda e, ls=ln_s, qs=q_scr: e.tensor_tensor(ls[:, 4:5], ls[:, 6:7], ls[:, 6:7], OP.mult))
            Vc(lambda e, ls=ln_s, qs=q_scr: e.tensor_tensor(ls[:, 4:5], ls[:, 4:5], ls[:, 3:4], OP.mult))
            Vc(lambda e, ls=ln_s, qs=q_scr: e.tensor_scalar(ls[:, 4:5], ls[:, 4:5], -0.5, 1.5, OP.mult, OP.add))
            Vc(lambda e, ls=ln_s, qs=q_scr: e.tensor_tensor(ls[:, 6:7], ls[:, 6:7], ls[:, 4:5], OP.mult))
            Vc(lambda e, x2=x_t, ls=ln_s, qs=q_scr: e.tensor_scalar(xn_t[:], x2[:], ls[:, 2:3], ls[:, 6:7],
                                                 OP.subtract, OP.mult))
            if tt >= 2:
                P.wait("vector", "xpk", xpk_after[tt - 2])
            Vc(lambda e, d=xnb_t, ls=ln_s, qs=q_scr: e.tensor_copy(d[:], xn_t[:]))
            xnb_cnt = P.c("vch")
            Vc(lambda e, ls=ln_s, qs=q_scr: e.reduce_max(ls[:, 7:8], xn_t[:], axis=AX,
                                      apply_absolute_value=True))
            Vc(lambda e, ls=ln_s, qs=q_scr: e.tensor_scalar(ls[:, 7:8], ls[:, 7:8], EPS_Q, None, OP.max))
            Vc(lambda e, ls=ln_s, qs=q_scr: e.reciprocal(ls[:, 8:9], ls[:, 7:8]))
            Vc(lambda e, ls=ln_s, qs=q_scr: e.tensor_scalar(ls[:, 9:10], ls[:, 8:9], 128.0, None, OP.mult))
            Vc(lambda e, ls=ln_s, qs=q_scr: e.tensor_scalar(ls[:, 10:11], ls[:, 7:8], 1.0 / 128.0, None, OP.mult))
            P.wait_cur("scalar", "vch")
            P.wait_cur("scalar", "c_ld")
            Sc(lambda e, ls=ln_s, qs=q_scr: e.activation(qs[:], xn_t[:], ACT.Identity,
                                      bias=MAG_AP, scale=ls[:, 9:10]))
            P.wait_cur("vector", "sch")
            Vc(lambda e, ls=ln_s, qs=q_scr: e.tensor_scalar(qs[:], qs[:], -MAGIC, None, OP.add))
            Vc(lambda e, ls=ln_s, qs=q_scr: e.tensor_scalar(qs[:], qs[:], 127.0, -128.0, OP.min, OP.max))
            Vc(lambda e, d=xqa_t, ls=ln_s, qs=q_scr: e.tensor_scalar(d[:], qs[:], ls[:, 10:11], None, OP.mult))
            P.wait("gpsimd", "vch", P.c("vch"))
            P.dma("gpsimd", xqatok_park[tt * 128:(tt + 1) * 128, :], xqa_t[:], sem="xpk")
            for _ in range(2):
                if abs_pieces_idx[0] < len(abs_pieces):
                    emit_abs_piece(abs_pieces_idx[0])
                    abs_pieces_idx[0] += 1
            xnsb = xns0 if par == 0 else xns1
            if tt == 0:
                P.wait("tensor", "c_ld", P.c("c_ld"))
            P.wait("tensor", "vch", xnb_cnt)
            P.wait("tensor", "xte", P.c("xte"))
            for c in range(16):
                T(lambda e, cc2=c, xb=xnb_t: e.transpose(
                    ps_tb[:, cc2 * 128:(cc2 + 1) * 128],
                    xb[:, cc2 * 128:(cc2 + 1) * 128], c_identb[:]), sem="xtp", n=1)
            P.wait("scalar", "xtp", P.c("xtp"))
            P.wait("scalar", "xns_st", P.c("xns_st"))
            for c in range(16):
                S(lambda e, cc2=c, b=xnsb: e.copy(
                    b[:, cc2 * 128:(cc2 + 1) * 128],
                    ps_tb[:, cc2 * 128:(cc2 + 1) * 128]), sem="xte", n=1)
            P.wait("gpsimd", "xte", P.c("xte"))
            xdst = cc_xnT_inA if tt < 4 else cc_xnT_inB
            tcol = (tt % 4) * 128
            P.dma("gpsimd",
                  xdst.rearrange("(c p) t -> p c t", p=128)[:, :, tcol:tcol + 128],
                  xnsb[:].rearrange("p (c t) -> p c t", c=16), sem="xns_st")
            if tt == 3:
                P.wait("gpsimd", "xns_st", P.c("xns_st"))
                G(lambda e: e.collective_compute(
                    "AllGather", OP.bypass,
                    ins=[cc_xnT_inA[:]], outs=[cc_xnT_outA[:]],
                    replica_groups=[list(range(NCORE))]), sem="cc1", n=1)
            vch_after_x[tt] = P.c("vch")
            sch_after_x[tt] = P.c("sch")
            xpk_after[tt] = P.c("xpk")

        while abs_pieces_idx[0] < len(abs_pieces):
            emit_abs_piece(abs_pieces_idx[0])
            abs_pieces_idx[0] += 1
        P.wait("vector", "abss", P.c("abss"))
        Vc(lambda e: e.reduce_sum(abs_acc[:, 16:17], abs_acc[:, 0:4], axis=AX))
        Vc(lambda e: e.reduce_sum(abs_acc[:, 17:18], abs_acc[:, 4:8], axis=AX))
        Vc(lambda e: e.reduce_sum(abs_acc[:, 18:19], abs_acc[:, 8:16], axis=AX))
        Vc(lambda e: e.memset(abs_acc[:, 19:20], 0.0))
        P.wait_cur("gpsimd", "vch")
        P.dma("gpsimd", cc_absw_in[:, :], abs_acc[:, 16:20], sem="pk")

        P.wait("gpsimd", "pk", P.c("pk"))
        G(lambda e: e.collective_compute(
            "AllGather", OP.bypass,
            ins=[cc_absw_in[:]], outs=[cc_absw_out[:]],
            replica_groups=[list(range(NCORE))]), sem="cc1", n=1)
        P.wait("gpsimd", "xns_st", P.c("xns_st"))
        G(lambda e: e.collective_compute(
            "AllGather", OP.bypass,
            ins=[cc_xnT_inB[:]], outs=[cc_xnT_outB[:]],
            replica_groups=[list(range(NCORE))]), sem="cc1", n=1)
        P.barrier("bar_p0")

    ph12 = ExitStack()
    sb12 = lambda nm, shp, dt=F32: ph12.enter_context(nc.sbuf_tensor(nm, shp, dt))
    q_all = sb12("q_all", [128, HOWN * NTOK], BF16)
    k_all = sb12("k_all", [128, HOWN * NTOK], BF16)
    v_all = sb12("v_all", [128, 64 * VCW], BF16)
    wq_b = sb12("wq_b", [128, 16 * FOWN], BF16)
    wk_b = sb12("wk_b", [128, 16 * FOWN], BF16)
    wv_b = sb12("wv_b", [128, 16 * FOWN], BF16)
    wo_b = sb12("wo_b", [128, 2 * D], BF16)
    twscr = sb12("twscr", [128, 2048])
    tgb = sb12("tgb", [128, 2048], BF16)
    ttb = sb12("ttb", [128, 2048], BF16)

    with ExitStack() as ph1:
        sb1 = lambda nm, shp, dt=F32: ph1.enter_context(nc.sbuf_tensor(nm, shp, dt))
        xstrip = sb1("xstrip", [128, 16 * 512], BF16)
        xstrip2 = sb1("xstrip2", [128, 16 * 512], BF16)
        cs_t = sb1("cs_t", [64, 2 * 512])
        cs_t2 = sb1("cs_t2", [64, 2 * 512])
        rp_scr = sb1("rp_scr", [64, 4 * 512])

        for wsrc, wdst in ((wqT_s, wq_b), (wkT_s, wk_b), (wvT_s, wv_b)):
            for half in range(2):
                P.wait("sync", "wcast", P.c("wcast"))
                P.dma("sync", twscr[:, 0:2048].rearrange("p (c f) -> p c f", c=8),
                      wsrc[half * 1024:(half + 1) * 1024, :].rearrange(
                          "(c p) f -> p c f", p=128), sem="wld")
                P.wait("vector", "wld", P.c("wld"))
                V(lambda e, d=wdst, h=half: e.tensor_copy(
                    d[:, h * 2048:(h + 1) * 2048], twscr[:, 0:2048]), sem="wcast", n=1)
        for c in range(2):
            P.wait("sync", "wcast", P.c("wcast"))
            P.dma("sync", twscr[:, 0:2048], woT_s[c * 128:(c + 1) * 128, :], sem="wld")
            P.wait("vector", "wld", P.c("wld"))
            V(lambda e, cc=c: e.tensor_copy(wo_b[:, cc * 2048:(cc + 1) * 2048],
                                            twscr[:, 0:2048]), sem="wcast", n=1)

        for g in range(2):
            Vc(lambda e, gg=g: e.memset(
                v_all[:].rearrange("p (c g e) -> p c g e", c=64, g=2)[:, :, gg, HD:HD + 1],
                1.0))

        P.wait("sync", "cc1", 2)
        P.dma("sync", abs_acc[:, 0:NCORE * 4].rearrange("p (r m) -> p r m", r=NCORE),
              cc_absw_out[:].rearrange("r p m -> p r m"), sem="aw_ld")
        P.wait("vector", "aw_ld", P.c("aw_ld"))
        Vc(lambda e: e.reduce_sum(
            abs_acc[:, 16:20],
            abs_acc[:, 0:NCORE * 4].rearrange("p (r m) -> p m r", r=NCORE), axis=AX))
        P.wait_cur("tensor", "vch")
        P.wait("tensor", "xte", P.c("xte"))
        T(lambda e: e.matmul(ps_o[:, 0:4], c_onesf[:], abs_acc[:, 16:20],
                             start=True, stop=True), sem="mm_mean", n=1)
        P.wait("vector", "mm_mean", P.c("mm_mean"))
        Vc(lambda e: e.memset(invN[:, 0:3], 1.0 / (D * NI)))
        Vc(lambda e: e.memset(invN[:, 3:4], 1.0))
        Vc(lambda e: e.tensor_copy(sc_mean[:, :], ps_o[:, 0:4]))
        Vc(lambda e: e.tensor_tensor(sc_mean[:, :], sc_mean[:, :], invN[:, :], OP.mult))
        Vc(lambda e: e.tensor_scalar(sc_mean[:, :], sc_mean[:, :], EPS_Q, None, OP.max))
        Vc(lambda e: e.reciprocal(sc_sw[:, 0:1], sc_mean[:, 0:1]))
        Vc(lambda e: e.reciprocal(sc_sw[:, 1:2], sc_mean[:, 1:2]))
        Vc(lambda e: e.reciprocal(sc_sw[:, 2:3], sc_mean[:, 2:3]))
        if "mean" in taps:
            P.wait_cur("gpsimd", "vch")
            P.dma("gpsimd", taps["mean"], sc_mean[:, :], sem="tapd")

        P.wait("tensor", "wcast", P.c("wcast"))
        qkv_mm_after = {}
        rpo_w_after = {}
        rpo_after = {}
        vev_par = {0: 0, 1: 0}
        for w in range(NTOK // 512):
            r, off = w // 2, (w % 2) * 512
            xs = xstrip if (w % 2 == 0) else xstrip2
            cs = cs_t if (w % 2 == 0) else cs_t2
            lsem = "xs_ld0" if (w % 2 == 0) else "xs_ld1"
            if w >= 2:
                P.wait("sync", "qkv_mm", qkv_mm_after[w - 2])
                P.wait("sync", "rpo", rpo_w_after[w - 2])
            P.wait("sync", "cc1", 1 if off == 0 else 3)
            xsrc = cc_xnT_outA if off == 0 else cc_xnT_outB
            P.dma("sync", xs[:].rearrange("p (c t) -> p c t", c=16),
                  xsrc[r, :, 0:512].rearrange("(c p) t -> p c t", p=128),
                  sem=lsem)
            P.dma("sync", cs[:, 0:512], cosT[:, w * 512:(w + 1) * 512], sem=lsem)
            P.dma("sync", cs[:, 512:1024], sinT[:, w * 512:(w + 1) * 512], sem=lsem)
            P.wait("tensor", lsem, P.c(lsem))
            P.wait("vector", lsem, P.c(lsem))
            for mat, wbuf, dstall, pp in ((0, wq_b, q_all, ps_a), (1, wk_b, k_all, ps_b)):
                for f2 in range(HOWN):
                    pdst = pp[:, f2 * 512:(f2 + 1) * 512]
                    P.wait("tensor", "rpo", rpo_after.get((w - 1, mat, f2)))
                    for fi in range(16):
                        T(lambda e, p=pdst, wb=wbuf, f=fi, ff=f2, x2=xs:
                          e.matmul(p, wb[:, f * FOWN + ff * 128:f * FOWN + ff * 128 + 128],
                                   x2[:, f * 512:(f + 1) * 512],
                                   start=(f == 0), stop=(f == 15)), sem="qkv_mm", n=1)
                    P.wait("vector", "qkv_mm", P.c("qkv_mm"))
                    P.wait("vector", "rpo", P.c("rpo"))
                    for i4, (rows, cx) in enumerate((((0, 64), (0, 512)),
                                                     ((64, 128), (512, 1024)),
                                                     ((0, 64), (512, 1024)),
                                                     ((64, 128), (0, 512)))):
                        V(lambda e, p=pdst, rw=rows, cc2=cx, i=i4, c3=cs: e.tensor_tensor(
                            rp_scr[:, i * 512:(i + 1) * 512],
                            p[rw[0]:rw[1], :], c3[:, cc2[0]:cc2[1]], OP.mult),
                          sem="rp", n=1)
                    dcol = f2 * NTOK + w * 512
                    P.wait_cur("vector", "rp")
                    V(lambda e, d=dstall, c2=dcol: e.tensor_tensor(
                        d[0:64, c2:c2 + 512], rp_scr[:, 0:512], rp_scr[:, 512:1024],
                        OP.subtract), sem="rpo", n=1)
                    V(lambda e, d=dstall, c2=dcol: e.tensor_tensor(
                        d[64:128, c2:c2 + 512], rp_scr[:, 1024:1536],
                        rp_scr[:, 1536:2048], OP.add), sem="rpo", n=1)
                    rpo_after[(w, mat, f2)] = P.c("rpo")
            for ts in range(4):
                par = ts % 2
                pv = ps_c[:, par * 512:par * 512 + 256]
                P.wait("tensor", "vev", vev_par[par])
                for fi in range(16):
                    T(lambda e, p=pv, f=fi, t=ts, x2=xs:
                      e.matmul(p, x2[:, f * 512 + t * 128:f * 512 + t * 128 + 128],
                               wv_b[:, f * FOWN:(f + 1) * FOWN],
                               start=(f == 0), stop=(f == 15)), sem="qkv_mm", n=1)
                P.wait("vector", "qkv_mm", P.c("qkv_mm"))
                P.wait("vector", "vch", P.c("vch"))
                tc = w * 4 + ts
                V(lambda e, p=pv, c2=tc: e.tensor_copy(
                    v_all[:, c2 * VCW:c2 * VCW + HD], p[:, 0:HD]), sem="vev", n=1)
                V(lambda e, p=pv, c2=tc: e.tensor_copy(
                    v_all[:, c2 * VCW + HD + 1:c2 * VCW + 2 * HD + 1],
                    p[:, HD:2 * HD]), sem="vev", n=1)
                vev_par[par] = P.c("vev")
            rpo_w_after[w] = P.c("rpo")
            qkv_mm_after[w] = P.c("qkv_mm")
        P.barrier("bar_qkv")

        if "q_all" in taps:
            for h in range(HOWN):
                P.dma("gpsimd", taps["q_all"][h], q_all[:, h * NTOK:(h + 1) * NTOK],
                      sem="tapd")
        if "k_all" in taps:
            for h in range(HOWN):
                P.dma("gpsimd", taps["k_all"][h], k_all[:, h * NTOK:(h + 1) * NTOK],
                      sem="tapd")
        if "v_all" in taps:
            P.dma("gpsimd", taps["v_all"], v_all[:, :], sem="tapd")
        if taps:
            P.wait("gpsimd", "tapd", P.c("tapd"))
            P.barrier("bar_tap1")

    tern_pieces = []
    tern_done = [0]
    for (src_dram, dst_dram, m_idx) in ((gT_s, cc_tg_in, 0), (uT_s, cc_tu_in, 1),
                                        (dT_s, cc_td_in, 2)):
        ntile = src_dram.shape[0] // 128
        for it in range(ntile):
            for piece in range(src_dram.shape[1] // 2048):
                tern_pieces.append((src_dram, dst_dram, m_idx, it, piece))

    def emit_tern_piece(spec):
        src_dram, dst_dram, m_idx, it, piece = spec
        P.wait("sync", "terv", P.c("terv"))
        P.dma("sync", twscr[:, 0:2048],
              src_dram[it * 128:(it + 1) * 128,
                       piece * 2048:(piece + 1) * 2048], sem="ter_ld")
        P.wait("vector", "ter_ld", P.c("ter_ld"))
        P.wait("vector", "ter_st", P.c("ter_st"))
        V(lambda e, m=m_idx: e.tensor_scalar(
            tgb[:, 0:2048], twscr[:, 0:2048], sc_sw[:, m:m + 1], 0.5,
            OP.mult, OP.is_ge), sem="terv", n=1)
        V(lambda e, m=m_idx: e.tensor_scalar(
            ttb[:, 0:2048], twscr[:, 0:2048], sc_sw[:, m:m + 1], -0.5,
            OP.mult, OP.is_le), sem="terv", n=1)
        P.wait_cur("vector", "terv")
        V(lambda e: e.tensor_tensor(
            ttb[:, 0:2048], tgb[:, 0:2048], ttb[:, 0:2048], OP.subtract),
          sem="terv", n=1)
        P.wait("gpsimd", "terv", P.c("terv"))
        P.dma("gpsimd", dst_dram[it * 128:(it + 1) * 128,
                                 piece * 2048:(piece + 1) * 2048],
              ttb[:, 0:2048], sem="ter_st")

    def emit_cc2_one(i):
        tin, tout = ((cc_tg_in, cc_tg_out), (cc_tu_in, cc_tu_out),
                     (cc_td_in, cc_td_out))[i]
        P.wait("gpsimd", "ter_st", P.c("ter_st"))
        G(lambda e, a=tin, o=tout: e.collective_compute(
            "AllGather", OP.bypass,
            ins=[a[:]], outs=[o[:]],
            replica_groups=[list(range(NCORE))]), sem="cc2", n=1)

    with ExitStack() as ph2:
        sb2 = lambda nm, shp, dt=F32: ph2.enter_context(nc.sbuf_tensor(nm, shp, dt))
        s2_sb = sb2("s2_sb", [128, 4 * D], BF16)
        s4_sb = sb2("s4_sb", [128, 4 * D], BF16)
        att_q = sb2("att_q", [128, 4 * FOWN], BF16)
        attT = sb2("attT", [128, 2 * QB], BF16)
        oep = sb2("oep", [128, 16])
        wo_ev = sb2("wo_ev", [128, 4 * D], BF16)

        p1_after = {}
        p2_after = {}
        av_after = {}
        atp_after = 0
        p1_all = p2_all = av_all = 0
        qb_order = list(range(0, NQB, 2)) + list(range(1, NQB, 2))
        units = [(qi, q, hh) for qi, q in enumerate(qb_order) for hh in range(HOWN)]
        p1_after = {}
        p2_after = {}
        p2_half = {}
        av_after = {}
        gctr = [0]
        tails = [None]

        def emit_scores(j, qb, h, g):
            qcol = h * NTOK + qb * 512
            slot = g % 4
            s2o = slot * D
            P.wait("tensor", "p1", max(0, p1_after.get(g - 1, 0) - 1))
            P.wait("scalar", "p2", p2_after.get(g - 4, 0))
            P.wait("vector", "avmm", av_after.get(g - 4, 0))
            for g2, sps in ((0, ps_a), (1, ps_b)):
                if g2 == 1:
                    P.wait("tensor", "p1", p1_after.get(g - 1, 0))
                for kc in (2 * g2, 2 * g2 + 1):
                    last = (kc == 2 * g2 + 1)
                    T(lambda e, p=sps, k=kc, jj=j, hh=h, qc=qcol:
                      e.matmul(p[:, (k % 2) * 512:(k % 2) * 512 + 512],
                               k_all[:, hh * NTOK + jj * 512 + k * 128:
                                     hh * NTOK + jj * 512 + k * 128 + 128],
                               q_all[:, qc:qc + 512],
                               start=True, stop=True),
                      sem=("smm" if last else None), n=2)
                P.wait("scalar", "smm", P.c("smm"))
                S(lambda e, p=sps, gg=g2, o=s2o: e.activation(
                    s2_sb[:, o + gg * 1024:o + gg * 1024 + 1024],
                    p[:, 0:1024], ACT.Square), sem="p1", n=1)
            p1_after[g] = P.c("p1")
            P.wait("vector", "p1", max(0, P.c("p1") - 1))
            if j == qb:
                V(lambda e, o=s2o: e.tensor_tensor(
                    s2_sb[:, o:o + 1024], s2_sb[:, o:o + 1024],
                    c_mask[:, 0:1024], OP.mult), sem="p2", n=1)
                P.wait_cur("vector", "p2")
            V(lambda e, o=s2o: e.tensor_tensor(
                s4_sb[:, o:o + 1024], s2_sb[:, o:o + 1024], s2_sb[:, o:o + 1024],
                OP.mult), sem="p2", n=1)
            p2_half[g] = P.c("p2")
            P.wait("vector", "p1", P.c("p1"))
            if j == qb:
                V(lambda e, o=s2o: e.tensor_tensor(
                    s2_sb[:, o + 1024:o + D], s2_sb[:, o + 1024:o + D],
                    c_mask[:, 1024:2048], OP.mult), sem="p2", n=1)
                P.wait_cur("vector", "p2")
            V(lambda e, o=s2o: e.tensor_tensor(
                s4_sb[:, o + 1024:o + D], s2_sb[:, o + 1024:o + D],
                s2_sb[:, o + 1024:o + D], OP.mult), sem="p2", n=1)
            p2_after[g] = P.c("p2")

        def emit_av(j, qb, h, g, first):
            slot = g % 4
            s2o = slot * D
            P.wait("tensor", "p2", p2_half[g])
            if first:
                P.wait("tensor", "vch", P.c("vch"))
                P.wait("tensor", "atpe", P.c("atpe"))
                P.wait("tensor", "woev", P.c("woev"))
            diag = (j == qb)
            nmm = 10 if diag else 16
            for kc in range(4):
                if kc == 2:
                    P.wait("tensor", "p2", p2_after[g])
                vbase = (j * 4 + kc) * VCW + h * (HD + 1)
                for k in range(4):
                    if diag and kc > k:
                        continue
                    last = (kc == 3 and k == 3)
                    stop = (j == qb and kc == min(3, k))
                    T(lambda e, kk=k, kcc=kc, o=s2o, vb=vbase, jj=j, st=stop:
                      e.matmul((ps_o if kk < 2 else ps_c)[:, (kk % 2) * 512:
                                                          (kk % 2) * 512 + 129],
                               s4_sb[:, o + kcc * 512 + kk * 128:
                                     o + kcc * 512 + kk * 128 + 128],
                               v_all[:, vb:vb + 129],
                               start=(jj == 0 and kcc == 0),
                               stop=st),
                      sem=("avmm" if last else None), n=nmm)
            av_after[g] = P.c("avmm")

        def mk_fin(qb, h, qbi):
            def fin():
                P.wait("vector", "avmm", P.c("avmm"))
                P.wait("vector", "atp", P.c("atp"))
                for k in range(4):
                    pav = (ps_o if k < 2 else ps_c)[:, (k % 2) * 512:(k % 2) * 512 + 129]
                    Vc(lambda e, p=pav: e.tensor_scalar(
                        oep[:, 0:1], p[:, 128:129], EPS_ATT, None, OP.max))
                    Vc(lambda e: e.reciprocal(oep[:, 8:9], oep[:, 0:1]))
                    Vc(lambda e, p=pav, kk=k, hh=h: e.tensor_scalar(
                        att_q[:, kk * FOWN + hh * HD:kk * FOWN + hh * HD + HD],
                        p[:, 0:128], oep[:, 8:9], None, OP.mult))
                P.wait("tensor", "vch", P.c("vch"))
                if tern_pieces:
                    emit_tern_piece(tern_pieces.pop(0))
                    tern_done[0] += 1
                    if tern_done[0] % 8 == 0:
                        emit_cc2_one(tern_done[0] // 8 - 1)
                if h != HOWN - 1:
                    return
                P.wait("vector", "womm", P.c("womm"))
                for fc in range(2):
                    for tc2 in range(4):
                        slot = (tc2 % 2) * 1024
                        if P.c("atpe") >= 2:
                            P.wait("tensor", "atpe", P.c("atpe") - 1)
                        T(lambda e, f=fc, t=tc2, sl=slot: e.transpose(
                            ps_tb[:, sl:sl + 128],
                            att_q[:, t * FOWN + f * 128:t * FOWN + f * 128 + 128],
                            c_identb[:]), sem="atp", n=1)
                        P.wait("vector", "atp", P.c("atp"))
                        V(lambda e, f=fc, t=tc2, sl=slot: e.tensor_copy(
                            attT[:, f * QB + t * 128:f * QB + t * 128 + 128],
                            ps_tb[:, sl:sl + 128]), sem="atpe", n=1)
                P.wait("tensor", "atpe", P.c("atpe"))
                for tc2 in range(4):
                    wop = tc2 * D
                    P.wait("tensor", "woev", P.c("woev"))
                    for fc in range(2):
                        for dt4 in range(4):
                            pww = (ps_c if dt4 < 2 else ps_o)[:, (dt4 % 2) * 512:(dt4 % 2) * 512 + 512]
                            T(lambda e, p=pww, f=fc, t=tc2, d=dt4:
                              e.matmul(p, attT[:, f * QB + t * 128:f * QB + t * 128 + 128],
                                       wo_b[:, f * D + d * 512:f * D + d * 512 + 512],
                                       start=(f == 0), stop=(f == 1)), sem="womm", n=1)
                    P.wait("scalar", "womm", P.c("womm"))
                    if P.c("wost") >= 64:
                        P.wait("scalar", "wost", P.c("wost") - 48)
                    for dt4 in range(4):
                        pww = (ps_c if dt4 < 2 else ps_o)[:, (dt4 % 2) * 512:(dt4 % 2) * 512 + 512]
                        S(lambda e, p=pww, d=dt4, w2=wop: e.activation(
                            wo_ev[:, w2 + d * 512:w2 + d * 512 + 512], p,
                            ACT.Identity), sem="woev", n=1)
                    P.wait("gpsimd", "woev", P.c("woev"))
                    dst_half = att_partA if (qb % 2 == 0) else att_partB
                    drow = (qb // 2) * 512 + tc2 * 128
                    P.dma("gpsimd", dst_half[drow:drow + 128, :],
                          wo_ev[:, wop:wop + D], sem="wost")
                if qbi == NQB // 2 - 1:
                    P.wait("gpsimd", "wost", P.c("wost"))
                    G(lambda e: e.collective_compute(
                        "ReduceScatter", OP.add,
                        ins=[att_partA[:]], outs=[rs_outA[:]],
                        replica_groups=[list(range(NCORE))]), sem="cc3", n=1)
            return fin

        DEFER = False
        for qbi, qb, h in units:
            g0 = gctr[0]
            emit_scores(0, qb, h, g0)
            t3 = tails[0]
            if t3 is not None and DEFER:
                t3[0]()
            if qb >= 1:
                emit_scores(1, qb, h, g0 + 1)
            if t3 is not None and DEFER:
                t3[1]()
                t3[2]()
                tails[0] = None
            for j in range(2, qb + 1):
                emit_av(j - 2, qb, h, g0 + j - 2, first=(j == 2))
                emit_scores(j, qb, h, g0 + j)
            gctr[0] += qb + 1
            if qb == 0:
                tails[0] = [lambda: None, lambda: None, None]
                tails[0][1] = (lambda q2=qb, h2=h, gg=g0: emit_av(0, q2, h2, gg, True))
                tails[0][2] = mk_fin(qb, h, qbi)
            else:
                first1 = (qb == 1)
                tails[0] = [
                    (lambda q2=qb, h2=h, gg=g0, f1=first1:
                     emit_av(q2 - 1, q2, h2, gg + q2 - 1, f1)),
                    (lambda q2=qb, h2=h, gg=g0: emit_av(q2, q2, h2, gg + q2, False)),
                    mk_fin(qb, h, qbi)]
            if not DEFER:
                t3 = tails[0]
                t3[0]()
                t3[1]()
                t3[2]()
                tails[0] = None
        if tails[0] is not None:
            t3 = tails[0]
            t3[0]()
            t3[1]()
            t3[2]()
            tails[0] = None

        P.wait("gpsimd", "wost", P.c("wost"))
        G(lambda e: e.collective_compute(
            "ReduceScatter", OP.add,
            ins=[att_partB[:]], outs=[rs_outB[:]],
            replica_groups=[list(range(NCORE))]), sem="cc3", n=1)
        P.barrier("bar_att")

    ph12.close()

    ph3y = ExitStack()
    yA = ph3y.enter_context(nc.sbuf_tensor("yA", [128, 64 * TW], BF16))
    with ExitStack() as ph3:
        sb3 = lambda nm, shp, dt=F32: ph3.enter_context(nc.sbuf_tensor(nm, shp, dt))
        xqaT = sb3("xqaT", [128, 16 * TOWN], BF16)
        gstrip = sb3("gstrip", [128, 16 * 128], BF16)
        gstrip2 = sb3("gstrip2", [128, 16 * 128], BF16)
        ustrip = sb3("ustrip", [128, 16 * 128], BF16)
        ustrip2 = sb3("ustrip2", [128, 16 * 128], BF16)
        sg_sb = sb3("sg_sb", [128, 2 * TOWN], BF16)
        yB = sb3("yB", [128, 64 * TW], BF16)
        bfs0 = sb3("bfs0a", [128, TW], BF16)
        bfs1 = sb3("bfs1a", [128, TW], BF16)

        bf6A = {}

        def emit_bfly_group(st, g0):
            dlt = 1 << (st - 1)
            P.wait("vector", "ypkv", P.c("ypkv"))
            for yb in (yA, yB):
                for off in range(dlt):
                    ai, bi = g0 + off, g0 + off + dlt
                    tb6 = bfs0 if (off % 2 == 0) else bfs1
                    P.wait("vector", "bf", max(0, P.c("bf") - 5))
                    V(lambda e, a=ai, b2=bi, tb=tb6, y=yb: e.tensor_tensor(
                        tb[:], y[:, a * TW:(a + 1) * TW],
                        y[:, b2 * TW:(b2 + 1) * TW], OP.add), sem="bf", n=1)
                    P.wait_cur("vector", "bf")
                    V(lambda e, a=ai, b2=bi, y=yb: e.tensor_tensor(
                        y[:, b2 * TW:(b2 + 1) * TW],
                        y[:, a * TW:(a + 1) * TW],
                        y[:, b2 * TW:(b2 + 1) * TW], OP.subtract), sem="bf", n=1)
                    P.wait_cur("vector", "bf")
                    V(lambda e, a=ai, tb=tb6, y=yb: e.tensor_copy(
                        y[:, a * TW:(a + 1) * TW], tb[:]), sem="bf", n=1)
                    if st == 6 and yb is yA:
                        bf6A[off] = P.c("bf")

        P.wait("sync", "xpk", P.c("xpk"))
        for c in range(16):
            P.op("sync", lambda e, cc2=c: e.dma_start(
                out=xqaT[:, cc2 * 1024:(cc2 + 1) * 1024],
                in_=xqatok_park[:, cc2 * 128:(cc2 + 1) * 128],
                transpose=True), sem="xq_ld", n=16)
        P.wait("tensor", "xq_ld", P.c("xq_ld"))
        if "xqaT" in taps:
            P.wait("gpsimd", "xq_ld", P.c("xq_ld"))
            P.dma("gpsimd", taps["xqaT"], xqaT[:, :], sem="tapd")
            P.wait("tensor", "tapd", P.c("tapd"))

        gu_after = {}
        sg_after = {}
        y_after = {}
        sgypk_after = {}
        for b in range(64):
            par = b % 2
            gs = gstrip if par == 0 else gstrip2
            us = ustrip if par == 0 else ustrip2
            psg = ps_a if par == 0 else ps_o
            psu = ps_b if par == 0 else ps_c
            sgb = sg_sb[:, 0:TOWN] if par == 0 else sg_sb[:, TOWN:2 * TOWN]
            lsem = "gu_ld0" if par == 0 else "gu_ld1"
            if b >= 2:
                P.wait("sync", "gu_mm", gu_after[b - 2])
            if b == 0:
                P.wait("sync", "cc2", 1)
            P.dma("sync", gs[:].rearrange("p (c i) -> p c i", c=16),
                  cc_tg_out[:, b * 128:(b + 1) * 128].rearrange(
                      "(c p) i -> p c i", p=128), sem=lsem)
            if b == 0:
                P.wait("sync", "cc2", 2)
            P.dma("sync", us[:].rearrange("p (c i) -> p c i", c=16),
                  cc_tu_out[:, b * 128:(b + 1) * 128].rearrange(
                      "(c p) i -> p c i", p=128), sem=lsem)
            P.wait("tensor", lsem, P.c(lsem))
            if b >= 2:
                P.wait("tensor", "sgs", sg_after[b - 2])
                P.wait("tensor", "ypkv", y_after[b - 2])
            for tt2 in range(2):
                for hc in range(16):
                    T(lambda e, g2=gs, h2=hc, t=tt2, p=psg:
                      e.matmul(p[:, t * 512:(t + 1) * 512],
                               g2[:, h2 * 128:(h2 + 1) * 128],
                               xqaT[:, h2 * TOWN + t * 512:h2 * TOWN + t * 512 + 512],
                               start=(h2 == 0), stop=(h2 == 15)), sem="gu_mm", n=1)
            for tt2 in range(2):
                for hc in range(16):
                    T(lambda e, u2=us, h2=hc, t=tt2, p=psu:
                      e.matmul(p[:, t * 512:(t + 1) * 512],
                               u2[:, h2 * 128:(h2 + 1) * 128],
                               xqaT[:, h2 * TOWN + t * 512:h2 * TOWN + t * 512 + 512],
                               start=(h2 == 0), stop=(h2 == 15)), sem="gu_mm", n=1)
            gu_after[b] = P.c("gu_mm")
            P.wait("scalar", "gu_mm", P.c("gu_mm"))
            if b >= 2:
                P.wait("scalar", "ypkv", sgypk_after[b - 2])
            S(lambda e, p=psg, s2=sgb: e.activation(
                s2, p[:, 0:1024], ACT.Silu,
                bias=ZERO_AP, scale=sc_mean[:, 0:1]), sem="sgs", n=1)
            sg_after[b] = P.c("sgs")
            P.wait("vector", "sgs", P.c("sgs"))
            V(lambda e, p=psu, s2=sgb, bb=b: e.scalar_tensor_tensor(
                yA[:, bb * TW:(bb + 1) * TW], p[:, 0:512],
                sc_mean[:, 1:2], s2[0:128, 0:512], OP.mult, OP.mult), sem="ypkv", n=1)
            V(lambda e, p=psu, s2=sgb, bb=b: e.scalar_tensor_tensor(
                yB[:, bb * TW:(bb + 1) * TW], p[:, 512:1024],
                sc_mean[:, 1:2], s2[0:128, 512:1024], OP.mult, OP.mult), sem="ypkv", n=1)
            y_after[b] = P.c("ypkv")
            sgypk_after[b] = P.c("ypkv")
            for st6 in range(1, 7):
                if (b + 1) % (1 << st6) == 0:
                    emit_bfly_group(st6, b + 1 - (1 << st6))
        P.wait("gpsimd", "bf", P.c("bf"))
        P.dma("gpsimd", ypark1[:].rearrange("c p t -> p c t"),
              yB[:].rearrange("p (c t) -> p c t", c=64), sem="ypk")
        for e in ENGS:
            P.wait(e, "ypk", P.c("ypk"))

    with ExitStack() as ph3b:
        sb3b = lambda nm, shp, dt=F32: ph3b.enter_context(nc.sbuf_tensor(nm, shp, dt))
        yhalf = yA
        zscr = sb3b("zscr", [128, NI], BF16)
        zq_T = sb3b("zq_T", [128, 64 * TW], BF16)
        zst = sb3b("zst", [128, 32])
        ztok = sb3b("ztok", [128, NI], BF16)
        gbc = sb3b("gbc", [128, 2048])
        zqe = sb3b("zqe", [128, 4 * 512])
        ffd_e = sb3b("ffd_e", [128, 512])
        dstrip = sb3b("dstrip", [128, 64 * 128], BF16)
        dstrip2 = zscr

        dn_after = {}
        dne_after = {}
        zq1_after = {}
        zq2_after = {}
        zq3_after = {}

        def load_y1():
            P.wait("sync", "zfmm", P.c("zfmm"))
            P.wait("sync", "ztmm", P.c("ztmm"))
            P.dma("sync", yhalf[:].rearrange("p (c t) -> p c t", c=64),
                  ypark1[:].rearrange("c p t -> p c t"), sem="yh_ld")

        def mk_stats_steps(half):
            zoff = half * 16
            goff = half * 1024
            steps = []

            def tc3_chunk(tc3):
                P.wait("tensor", "yh_ld", P.c("yh_ld"))
                P.wait("tensor", "bf", P.c("bf"))
                P.wait("tensor", "gbv", P.c("gbv"))
                P.wait("scalar", "zs", P.c("zs"))
                for b in range(64):
                    sl = b % 4
                    pz = (ps_a if sl < 2 else ps_o)[:, (sl % 2) * 512:(sl % 2) * 512 + 128]
                    if P.c("ztev") >= 4:
                        P.wait("tensor", "ztev", P.c("ztev") - 3)
                    T(lambda e, bb=b, t=tc3, p=pz: e.matmul(
                        p, yhalf[:, bb * TW + t * 128:bb * TW + t * 128 + 128],
                        c_H128[:], start=True, stop=True), sem="ztmm", n=1)
                    P.wait("scalar", "ztmm", P.c("ztmm"))
                    S(lambda e, bb=b, p=pz: e.copy(
                        ztok[:, bb * 128:(bb + 1) * 128], p), sem="ztev", n=1)
                P.wait("vector", "ztev", P.c("ztev"))
                V(lambda e, t=tc3: e.reduce_max(
                    zst[:, zoff + t:zoff + t + 1], ztok[:, :], axis=AX,
                    apply_absolute_value=True), sem="zs", n=1)

            for t in range(4):
                steps.append(lambda tt3=t: tc3_chunk(tt3))

            def stat_combine():
                P.wait("vector", "zs", P.c("zs"))
                P.wait("vector", "grt", P.c("grt"))
                Vc(lambda e: e.tensor_scalar(
                    zst[:, zoff + 4:zoff + 8], zst[:, zoff:zoff + 4], C_FWHT,
                    EPS_Q, OP.mult, OP.max))
                for i in range(4):
                    Vc(lambda e, ii=i: e.reciprocal(
                        zst[:, zoff + 8 + ii:zoff + 9 + ii],
                        zst[:, zoff + 4 + ii:zoff + 5 + ii]))
                Vc(lambda e: e.tensor_scalar(
                    zst[:, zoff + 8:zoff + 12], zst[:, zoff + 8:zoff + 12],
                    128.0 * C_FWHT, None, OP.mult))
                Vc(lambda e: e.tensor_scalar(
                    zst[:, zoff + 12:zoff + 16], zst[:, zoff + 4:zoff + 8],
                    sc_mean[:, 2:3], None, OP.mult))
                Vc(lambda e: e.tensor_scalar(
                    zst[:, zoff + 12:zoff + 16], zst[:, zoff + 12:zoff + 16],
                    1.0 / 128.0, None, OP.mult))
                P.wait("gpsimd", "vch", P.c("vch"))
                P.wait("gpsimd", "gbc_ld", P.c("gbc_ld"))
                P.dma("gpsimd", gam_rt[0, :].rearrange("(c p) -> p c", p=128),
                      zst[:, zoff + 8:zoff + 12], sem="grt", slow=True)
                P.dma("gpsimd", gam_rt[1, :].rearrange("(c p) -> p c", p=128),
                      zst[:, zoff + 12:zoff + 16], sem="grt", slow=True)
            steps.append(stat_combine)

            def gbc_mm():
                P.wait("sync", "grt", P.c("grt"))
                P.dma("sync", gbc[0:1, goff:goff + 512], gam_rt[0:1, :],
                      sem="gbc_ld")
                P.dma("sync", gbc[0:1, goff + 512:goff + 1024], gam_rt[1:2, :],
                      sem="gbc_ld")
                P.wait("tensor", "gbc_ld", P.c("gbc_ld"))
                P.wait("tensor", "gbv", P.c("gbv"))
                P.wait("tensor", "ztev", P.c("ztev"))
                T(lambda e: e.matmul(ps_o[:, 0:512], c_onesf[0:1, :],
                                     gbc[0:1, goff:goff + 512],
                                     start=True, stop=True), sem="gbmm", n=1)
                T(lambda e: e.matmul(ps_o[:, 512:1024], c_onesf[0:1, :],
                                     gbc[0:1, goff + 512:goff + 1024],
                                     start=True, stop=True), sem="gbmm", n=1)
                P.wait("vector", "gbmm", P.c("gbmm"))
                V(lambda e: e.tensor_copy(gbc[:, goff:goff + 512],
                                          ps_o[:, 0:512]), sem="gbv", n=1)
                V(lambda e: e.tensor_copy(gbc[:, goff + 512:goff + 1024],
                                          ps_o[:, 512:1024]), sem="gbv", n=1)
            steps.append(gbc_mm)
            return steps

        def sq_quant(half):
            goff = half * 1024
            P.wait("vector", "gbv", P.c("gbv"))
            P.wait("vector", "dnmm", P.c("dnmm"))
            P.wait("tensor", "dne", P.c("dne"))
            P.wait("tensor", "zq1", P.c("zq1"))
            P.wait("tensor", "ztev", P.c("ztev"))
            for b in range(64):
                sl = b % 4
                pz = (ps_a if sl < 2 else ps_b)[:, (sl % 2) * 512:(sl % 2) * 512 + 512]
                qa = zqe[:, (b % 2) * 1024:(b % 2) * 1024 + 512]
                qb2 = zqe[:, (b % 2) * 1024 + 512:(b % 2) * 1024 + 1024]
                if b >= 4:
                    P.wait("tensor", "zq1", zq1_after[b - 4])
                T(lambda e, bb=b, p=pz: e.matmul(
                    p, c_H128[:], yhalf[:, bb * TW:(bb + 1) * TW],
                    start=True, stop=True), sem="zfmm", n=1)
                P.wait("vector", "zfmm", P.c("zfmm"))
                if b >= 2:
                    P.wait("vector", "zq2", zq2_after[b - 2])
                V(lambda e, p=pz, q=qa: e.scalar_tensor_tensor(
                    q, p, 1.0, gbc[:, goff:goff + 512], OP.mult, OP.mult),
                  sem="zq1", n=1)
                zq1_after[b] = P.c("zq1")
                P.wait("vector", "zq1", P.c("zq1"))
                if b >= 2:
                    P.wait("vector", "zq3", zq3_after[b - 2])
                V(lambda e, q=qa, q2=qb2: e.tensor_scalar(
                    q2, q, MAGIC, -MAGIC, OP.add, OP.add), sem="zq2", n=1)
                zq2_after[b] = P.c("zq2")
                P.wait("vector", "zq2", P.c("zq2"))
                V(lambda e, bb=b, q2=qb2: e.tensor_scalar(
                    zq_T[:, bb * TW:(bb + 1) * TW], q2, 127.0, -128.0,
                    OP.min, OP.max), sem="zq3", n=1)
                zq3_after[b] = P.c("zq3")
            P.wait("tensor", "zq3", P.c("zq3"))

        def down(half, fill=()):
            goff = half * 1024
            fill = list(fill)
            fidx = [0]
            for dc in range(16):
                par = dc % 2
                ds = dstrip if par == 0 else dstrip2
                dsem = "dn_ld0" if par == 0 else "dn_ld1"
                prev = dn_after.get((half, dc - 2), dn_after.get((1 - half, 14 + dc), 0))
                P.wait("sync", "dnmm", prev)
                if half == 0 and dc == 0:
                    P.wait("sync", "cc2", 3)
                P.dma("sync", ds[:].rearrange("p (c i) -> p c i", c=64),
                      cc_td_out[:, dc * 128:(dc + 1) * 128].rearrange(
                          "(c p) i -> p c i", p=128), sem=dsem)
                P.wait("tensor", dsem, P.c(dsem))
                if dc >= 2:
                    P.wait("tensor", "dne", dne_after[(half, dc - 2)])
                elif half == 1:
                    P.wait("tensor", "dne", dne_after[(0, 14 + dc)])
                for ic in range(64):
                    T(lambda e, d=ds, i=ic, dcc=dc: e.matmul(
                        ps_b[:, (dcc % 2) * 512:(dcc % 2) * 512 + 512],
                        d[:, i * 128:(i + 1) * 128],
                        zq_T[:, i * TW:(i + 1) * TW],
                        start=(i == 0), stop=(i == 63)), sem="dnmm", n=1)
                dn_after[(half, dc)] = P.c("dnmm")
                P.wait("vector", "dnmm", P.c("dnmm"))
                P.wait("vector", "dtp", P.c("dtp"))
                P.wait("vector", "dst", P.c("dst"))
                V(lambda e, dcc=dc: e.scalar_tensor_tensor(
                    ffd_e[:, :], ps_b[:, (dcc % 2) * 512:(dcc % 2) * 512 + 512], 1.0,
                    gbc[:, goff + 512:goff + 1024], OP.mult, OP.mult), sem="dne", n=1)
                dne_after[(half, dc)] = P.c("dne")
                P.wait("tensor", "dne", P.c("dne"))
                P.wait("tensor", "dte", P.c("dte"))
                for t4 in range(4):
                    T(lambda e, t=t4: e.transpose(
                        ps_c[:, t * 128:t * 128 + 128],
                        ffd_e[:, t * 128:(t + 1) * 128], c_identf[:]), sem="dtp", n=1)
                P.wait("scalar", "dtp", P.c("dtp"))
                P.wait("scalar", "dst", P.c("dst"))
                P.wait("scalar", "dne", P.c("dne"))
                S(lambda e: e.copy(ffd_e[:, 0:512], ps_c[:, 0:512]), sem="dte", n=1)
                P.wait("gpsimd", "dte", P.c("dte"))
                for t4 in range(4):
                    P.dma("gpsimd",
                          ffn_park[half * 512 + t4 * 128:half * 512 + t4 * 128 + 128,
                                   dc * 128:(dc + 1) * 128],
                          ffd_e[:, t4 * 128:(t4 + 1) * 128], sem="dst")
                if fill and dc >= 4 and dc % 2 == 0 and fidx[0] < len(fill):
                    fill[fidx[0]]()
                    fidx[0] += 1
            while fidx[0] < len(fill):
                fill[fidx[0]]()
                fidx[0] += 1

        for st in mk_stats_steps(0):
            st()
        sq_quant(0)
        load_y1()
        down(0, fill=mk_stats_steps(1))
        sq_quant(1)
        down(1)
        if "ffn" in taps:
            P.wait("gpsimd", "dst", P.c("dst"))
            P.dma("gpsimd", taps["ffn"], ffn_park[:], sem="tapd")
        P.barrier("bar_ffn")
    ph3y.close()


    with ExitStack() as ph4:
        sb4 = lambda nm, shp, dt=F32: ph4.enter_context(nc.sbuf_tensor(nm, shp, dt))
        fin_x = sb4("fin_x", [128, D])
        fin_x2 = sb4("fin_x2", [128, D])
        fin_r = sb4("fin_r", [128, D], BF16)
        fin_r2 = sb4("fin_r2", [128, D], BF16)
        fin_f = sb4("fin_f", [128, D])
        fin_f2 = sb4("fin_f2", [128, D])
        fin_o = sb4("fin_o", [128, 4 * D])
        fin_o2 = None

        P.wait("sync", "cc3", P.c("cc3"))
        P.wait("sync", "dst", P.c("dst"))
        fo_after = {}
        for tt in range(8):
            par5 = tt % 2
            fx_t = fin_x[:, 0:D] if par5 == 0 else fin_x2[:, 0:D]
            fr_t = fin_r[:, 0:D] if par5 == 0 else fin_r2[:, 0:D]
            ff_t = fin_f[:, 0:D] if par5 == 0 else fin_f2[:, 0:D]
            fo_t = fin_o[:, (tt % 4) * D:(tt % 4) * D + D]
            if tt >= 2:
                P.wait("sync", "fo", fo_after[tt - 2])
            P.dma("sync", fx_t, x_own[tt * 128:(tt + 1) * 128, :], sem="fx")
            rsrc = rs_outA if tt < 4 else rs_outB
            roff = (tt % 4) * 128
            P.dma("sync", fr_t, rsrc[roff:roff + 128, :], sem="fx")
            P.dma("sync", ff_t, ffn_park[tt * 128:(tt + 1) * 128, :], sem="fx")
            P.wait("vector", "fx", P.c("fx"))
            if tt >= 4:
                P.wait("vector", "fst", 16 * (tt - 3))
            V(lambda e, a=fx_t, b=fr_t, o=fo_t: e.tensor_tensor(o, a, b, OP.add),
              sem="fo", n=1)
            P.wait_cur("vector", "fo")
            V(lambda e, o=fo_t, f=ff_t: e.tensor_tensor(o, o, f, OP.add),
              sem="fo", n=1)
            fo_after[tt] = P.c("fo")
            P.wait("gpsimd", "fo", P.c("fo"))
            P.dma("gpsimd", out_own[tt * 128:(tt + 1) * 128, :], fo_t, sem="fst")
        P.wait("gpsimd", "fst", P.c("fst"))
        if taps:
            P.wait("gpsimd", "tapd", P.c("tapd"))
        for s in sorted(P.cnt.keys()):
            P.wait("gpsimd", s, P.cnt[s])

        P.replay(blk)

    ctx.close()
    return nc



_consts = None


def _get_consts():
    global _consts
    if _consts is None:
        _consts = _host_consts()
    return _consts


def make_in_maps(L0_attn_x, L0_attn_Wq, L0_attn_Wk, L0_attn_Wv, L0_attn_Wo,
                 L0_ffn_w_gate, L0_ffn_w_up, L0_ffn_w_down):
    c = _get_consts()
    x = np.ascontiguousarray(np.asarray(L0_attn_x, np.float32).reshape(NTOK, D))
    wqT = np.asarray(L0_attn_Wq, np.float32).T
    wkT = np.asarray(L0_attn_Wk, np.float32).T
    wvT = np.asarray(L0_attn_Wv, np.float32).T
    woT = np.asarray(L0_attn_Wo, np.float32).T
    gT = np.asarray(L0_ffn_w_gate, np.float32).T
    uT = np.asarray(L0_ffn_w_up, np.float32).T
    dT = np.asarray(L0_ffn_w_down, np.float32).T
    in_maps = []
    for cc in range(NCORE):
        in_maps.append(dict(
            x_own=np.ascontiguousarray(x[cc * TOWN:(cc + 1) * TOWN, :]),
            wqT_s=np.ascontiguousarray(wqT[:, cc * FOWN:(cc + 1) * FOWN]),
            wkT_s=np.ascontiguousarray(wkT[:, cc * FOWN:(cc + 1) * FOWN]),
            wvT_s=np.ascontiguousarray(wvT[:, cc * FOWN:(cc + 1) * FOWN]),
            woT_s=np.ascontiguousarray(woT[cc * FOWN:(cc + 1) * FOWN, :]),
            gT_s=np.ascontiguousarray(gT[cc * FOWN:(cc + 1) * FOWN, :]),
            uT_s=np.ascontiguousarray(uT[cc * FOWN:(cc + 1) * FOWN, :]),
            dT_s=np.ascontiguousarray(dT[cc * TOWN:(cc + 1) * TOWN, :]),
            cosT=c["cosT"], sinT=c["sinT"], H128b=c["H128b"],
            identb=c["identb"], identf=c["identf"], onesf=c["onesf"],
            maskb=c["maskb"],
        ))
    return in_maps


def kernel(L0_attn_x, L0_attn_Wq, L0_attn_Wk, L0_attn_Wv, L0_attn_Wo,
           L0_ffn_w_gate, L0_ffn_w_up, L0_ffn_w_down):
    in_maps = make_in_maps(L0_attn_x, L0_attn_Wq, L0_attn_Wk, L0_attn_Wv,
                           L0_attn_Wo, L0_ffn_w_gate, L0_ffn_w_up, L0_ffn_w_down)
    nc = build_program()
    res = run_bass_kernel_spmd(nc, in_maps, core_ids=list(range(NCORE)))
    out = np.concatenate([np.asarray(r["out_own"], np.float32)
                          for r in res.results], axis=0)
    return out.reshape(1, NTOK, D)


# version 34
# speedup vs baseline: 1.0140x; 1.0140x over previous
import numpy as np
import ml_dtypes
from contextlib import ExitStack

import concourse.bass as bass
import concourse.mybir as mybir
from concourse.bass_utils import run_bass_kernel_spmd

F32 = mybir.dt.float32
BF16 = mybir.dt.bfloat16
AX = mybir.AxisListType.X
OP = mybir.AluOpType
ACT = mybir.ActivationFunctionType
NPBF = ml_dtypes.bfloat16

NTOK = 8192
D = 2048
NH = 16
HD = 128
NI = 8192
NCORE = 8
TOWN = NTOK // NCORE
HOWN = NH // NCORE
FOWN = HOWN * HD
QB = 512
NQB = NTOK // QB
TW = TOWN // 2
VCW = 2 * (HD + 1)
MAGIC = float(np.float32(3 * (2 ** 22)))
EPS_LN = 1e-5
EPS_ATT = 1e-6
EPS_Q = 1e-5
C_FWHT = float(np.float32(1.0 / np.sqrt(np.float32(8192.0))))

ENGS = ("sync", "vector", "scalar", "tensor", "gpsimd")


def _hadamard(m):
    h = np.array([[1.0]], np.float32)
    while h.shape[0] < m:
        h = np.block([[h, h], [h, -h]])
    return h.astype(np.float32)


def _host_consts():
    ident = np.eye(128, dtype=np.float32)
    theta = (10000.0 ** (-np.arange(64, dtype=np.float32) * 2.0 / 128.0)).astype(np.float32)
    pos = np.arange(NTOK, dtype=np.float32)
    ang = pos[None, :] * theta[:, None]
    masks = np.zeros((128, 4, QB), np.float32)
    for kc in range(4):
        kt = kc * 128 + np.arange(128)[:, None]
        qt = np.arange(QB)[None, :]
        masks[:, kc, :] = (kt <= qt).astype(np.float32)
    return dict(H128b=_hadamard(128).astype(NPBF),
                identb=ident.astype(NPBF), identf=ident,
                cosT=np.cos(ang).astype(np.float32),
                sinT=np.sin(ang).astype(np.float32),
                maskb=masks.reshape(128, 4 * QB).astype(NPBF),
                onesf=np.ones((128, 128), np.float32))


class Prog:

    def __init__(self, nc):
        self.nc = nc
        self.q = {e: [] for e in ENGS}
        self._stack = ExitStack()
        self.sems = {}
        self.cnt = {}

    def sem(self, name):
        if name not in self.sems:
            self.sems[name] = self._stack.enter_context(self.nc.semaphore(name))
            self.cnt[name] = 0
        return self.sems[name]

    def c(self, name):
        return self.cnt.get(name, 0)

    def op(self, eng, fn, sem=None, n=1):
        if sem is not None:
            self.sem(sem)
            self.cnt[sem] += n
        self.q[eng].append(("op", fn, sem, n))
        return self.cnt.get(sem)

    def wait(self, eng, sem, val):
        if val is None or val <= 0:
            return
        self.sem(sem)
        self.q[eng].append(("wait", sem, val))

    def wait_cur(self, eng, sem):
        self.wait(eng, sem, self.c(sem))

    def dma(self, eng, out, in_, sem=None, slow=False):
        if slow:
            return self.op(eng, lambda e, o=out, i=in_: e.dma_start(
                out=o, in_=i, allow_slow_non_contiguous=True), sem=sem, n=16)
        return self.op(eng, lambda e, o=out, i=in_: e.dma_start(out=o, in_=i),
                       sem=sem, n=16)

    def barrier(self, name):
        self.sem(name)
        self.cnt[name] += len(ENGS)
        tgt = self.cnt[name]
        for e in ENGS:
            self.q[e].append(("op", lambda eng: eng.nop(), name, 1))
        for e in ENGS:
            self.q[e].append(("wait", name, tgt))

    def replay(self, blk):
        def mk(eng_name):
            def run(e):
                for item in self.q[eng_name]:
                    if item[0] == "op":
                        _, fn, semname, n = item
                        inst = fn(e)
                        if semname is not None:
                            inst.then_inc(self.sems[semname], n)
                    else:
                        _, semname, val = item
                        e.wait_ge(self.sems[semname], val)
            return run
        blk.sync(mk("sync"))
        blk.vector(mk("vector"))
        blk.scalar(mk("scalar"))
        blk.tensor(mk("tensor"))
        blk.gpsimd(mk("gpsimd"))

    def close(self):
        self._stack.close()


def build_program(debug_taps=()):
    nc = bass.Bass("TRN2", target_bir_lowering=False, debug=False)

    def inp(name, shape, dt=F32):
        return nc.dram_tensor(name, shape, dt, kind="ExternalInput").ap()

    x_own = inp("x_own", [TOWN, D])
    wqT_s = inp("wqT_s", [D, FOWN])
    wkT_s = inp("wkT_s", [D, FOWN])
    wvT_s = inp("wvT_s", [D, FOWN])
    woT_s = inp("woT_s", [FOWN, D])
    gT_s = inp("gT_s", [FOWN, NI])
    uT_s = inp("uT_s", [FOWN, NI])
    dT_s = inp("dT_s", [TOWN, D])
    cosT = inp("cosT", [64, NTOK])
    sinT = inp("sinT", [64, NTOK])
    H128b = inp("H128b", [128, 128], BF16)
    identb = inp("identb", [128, 128], BF16)
    identf = inp("identf", [128, 128])
    onesf = inp("onesf", [128, 128])
    maskb = inp("maskb", [128, 4 * QB], BF16)

    out_own = nc.dram_tensor("out_own", [TOWN, D], F32, kind="ExternalOutput").ap()

    def idram(name, shape, dt=F32, shared=False):
        return nc.dram_tensor(name, shape, dt, kind="Internal",
                              addr_space=("Shared" if shared else "Local")).ap()

    cc_xnT_inA = idram("cc_xnT_inA", [D, TOWN // 2], BF16)
    cc_xnT_inB = idram("cc_xnT_inB", [D, TOWN // 2], BF16)
    cc_absw_in = idram("cc_absw_in", [128, 4], F32)
    cc_xnT_outA = idram("cc_xnT_outA", [NCORE, D, TOWN // 2], BF16, shared=True)
    cc_xnT_outB = idram("cc_xnT_outB", [NCORE, D, TOWN // 2], BF16, shared=True)
    cc_absw_out = idram("cc_absw_out", [NCORE, 128, 4], F32, shared=True)
    cc_tg_in = idram("cc_tg_in", [FOWN, NI], BF16)
    cc_tu_in = idram("cc_tu_in", [FOWN, NI], BF16)
    cc_td_in = idram("cc_td_in", [TOWN, D], BF16)
    cc_tg_out = idram("cc_tg_out", [D, NI], BF16, shared=True)
    cc_tu_out = idram("cc_tu_out", [D, NI], BF16, shared=True)
    cc_td_out = idram("cc_td_out", [NI, D], BF16, shared=True)
    att_partA = idram("att_partA", [NTOK // 2, D], BF16)
    att_partB = idram("att_partB", [NTOK // 2, D], BF16)
    rs_outA = idram("rs_outA", [TOWN // 2, D], BF16)
    rs_outB = idram("rs_outB", [TOWN // 2, D], BF16)
    xtok_park = idram("xtok_park", [TOWN, D], BF16)
    xqatok_park = idram("xqatok_park", [TOWN, D], BF16)
    ypark1 = idram("ypark1", [64, 128, TW], BF16)
    ffn_park = idram("ffn_park", [TOWN, D], F32)
    gam_rt = idram("gam_rt", [2, 512], F32)

    taps = {}
    for name, shape, dt in debug_taps:
        taps[name] = nc.dram_tensor("tap_" + name, shape, dt,
                                    kind="ExternalOutput").ap()

    ctx = ExitStack()
    sb = lambda nm, shp, dt=F32: ctx.enter_context(nc.sbuf_tensor(nm, shp, dt))
    psum = lambda nm, shp, dt=F32: ctx.enter_context(nc.psum_tensor(nm, shp, dt))

    c_H128 = sb("c_H128", [128, 128], BF16)
    c_identb = sb("c_identb", [128, 128], BF16)
    c_identf = sb("c_identf", [128, 128])
    c_onesf = sb("c_onesf", [128, 128])
    c_mask = sb("c_mask", [128, 4 * QB], BF16)
    c_sc = sb("c_sc", [128, 4])
    sc_mean = sb("sc_mean", [128, 4])
    sc_sw = sb("sc_sw", [128, 4])
    invN = sb("invN", [128, 4])
    abs_acc = sb("abs_acc", [128, 32])

    ps_a = psum("ps_a", [128, 1024])
    ps_b = psum("ps_b", [128, 1024])
    ps_o = psum("ps_o", [128, 1024])
    ps_c = psum("ps_c", [128, 1024])
    ps_tb = ps_o[:].bitcast(BF16)
    ps_scA = ps_a[:].bitcast(BF16)
    ps_scB = ps_b[:].bitcast(BF16)

    blk = ctx.enter_context(nc.Block())
    P = Prog(nc)
    ctx.callback(P.close)

    def T(fn, sem=None, n=1):
        return P.op("tensor", fn, sem, n)

    def V(fn, sem=None, n=1):
        return P.op("vector", fn, sem, n)

    def S(fn, sem=None, n=1):
        return P.op("scalar", fn, sem, n)

    def G(fn, sem=None, n=1):
        return P.op("gpsimd", fn, sem, n)

    def Vc(fn):
        P.wait_cur("vector", "vch")
        return V(fn, sem="vch", n=1)

    def Sc(fn):
        P.wait_cur("scalar", "sch")
        return S(fn, sem="sch", n=1)

    MAG_AP = c_sc[:, 0:1]
    ZERO_AP = c_sc[:, 1:2]

    with ExitStack() as ph0:
        sb0 = lambda nm, shp, dt=F32: ph0.enter_context(nc.sbuf_tensor(nm, shp, dt))
        x_t0 = sb0("x_t0", [128, D])
        x_t1 = sb0("x_t1", [128, D])
        ln_sA = sb0("ln_sA", [128, 16])
        ln_sB = sb0("ln_sB", [128, 16])
        xn_t = sb0("xn_t", [128, D])
        xnb_t0 = sb0("xnb_t0", [128, D], BF16)
        xnb_t1 = sb0("xnb_t1", [128, D], BF16)
        xqa_t0p = sb0("xqa_t0p", [128, D], BF16)
        xqa_t1p = sb0("xqa_t1p", [128, D], BF16)
        q_scr0 = sb0("q_scr0", [128, D])
        q_scr1 = sb0("q_scr1", [128, D])
        xns0 = sb0("xns0", [128, 2048], BF16)
        xns1 = sb0("xns1", [128, 2048], BF16)
        wscr = sb0("wscr", [128, 4096])
        wjunk = sb0("wjunk", [128, 4096])
        wjunk2 = sb0("wjunk2", [128, 4096])

        P.dma("sync", c_H128[:], H128b, sem="c_ld")
        P.dma("sync", c_identb[:], identb, sem="c_ld")
        P.dma("sync", c_identf[:], identf, sem="c_ld")
        P.dma("sync", c_onesf[:], onesf, sem="c_ld")
        P.dma("sync", c_mask[:], maskb, sem="c_ld")
        Vc(lambda e: e.memset(c_sc[:, 0:1], MAGIC))
        Vc(lambda e: e.memset(c_sc[:, 1:2], 0.0))
        Vc(lambda e: e.memset(abs_acc[:, 0:16], 0.0))

        abs_pieces = []
        _col = 0
        for (wsrc, ntile, width) in ((gT_s, 2, NI), (uT_s, 2, NI), (dT_s, 8, D)):
            for it in range(ntile):
                for piece in range(max(1, width // 4096)):
                    abs_pieces.append((wsrc, it, piece, min(4096, width), _col))
                    _col += 1
        abs_sc_after = {}

        def emit_abs_piece(idx):
            wsrc, it, piece, pw, c2 = abs_pieces[idx]
            par2 = idx % 2
            wbuf = wscr if par2 == 0 else wjunk
            if idx >= 2:
                P.wait("sync", "abss", abs_sc_after[idx - 2])
            P.dma("sync", wbuf[:, 0:pw],
                  wsrc[it * 128:(it + 1) * 128,
                       piece * 4096:piece * 4096 + pw], sem="abs_ld")
            P.wait("scalar", "abs_ld", P.c("abs_ld"))
            P.wait("scalar", "abss", P.c("abss"))
            S(lambda e, w=pw, cc3=c2, b=wbuf: e.activation(
                wjunk2[:, 0:w], b[:, 0:w], ACT.Abs,
                accum_out=abs_acc[:, cc3:cc3 + 1]), sem="abss", n=1)
            abs_sc_after[idx] = P.c("abss")

        abs_pieces_idx = [0]
        vch_after_x = {}
        sch_after_x = {}
        xpk_after = {}
        for tt in range(8):
            par = tt % 2
            x_t = x_t0 if par == 0 else x_t1
            xnb_t = xnb_t0 if par == 0 else xnb_t1
            xqa_t = xqa_t0p if par == 0 else xqa_t1p
            ln_s = ln_sA if par == 0 else ln_sB
            q_scr = q_scr0 if par == 0 else q_scr1
            if tt >= 2:
                P.wait("sync", "vch", vch_after_x[tt - 2])
                P.wait("sync", "sch", sch_after_x[tt - 2])
            P.dma("sync", x_t[:], x_own[tt * 128:(tt + 1) * 128, :], sem="x_ld")
            P.wait("vector", "x_ld", P.c("x_ld"))
            P.wait("scalar", "x_ld", P.c("x_ld"))
            if tt >= 2:
                P.wait("scalar", "vch", vch_after_x[tt - 2])
            Vc(lambda e, x2=x_t, ls=ln_s, qs=q_scr: e.reduce_sum(ls[:, 0:1], x2[:], axis=AX))
            Sc(lambda e, x2=x_t, ls=ln_s, qs=q_scr: e.activation(qs[:], x2[:], ACT.Square,
                                              accum_out=ls[:, 1:2]))
            P.wait_cur("vector", "sch")
            Vc(lambda e, ls=ln_s, qs=q_scr: e.tensor_scalar(ls[:, 2:3], ls[:, 0:1], 1.0 / D, None, OP.mult))
            Vc(lambda e, ls=ln_s, qs=q_scr: e.tensor_scalar(ls[:, 3:4], ls[:, 1:2], 1.0 / D, None, OP.mult))
            Vc(lambda e, ls=ln_s, qs=q_scr: e.tensor_tensor(ls[:, 4:5], ls[:, 2:3], ls[:, 2:3], OP.mult))
            Vc(lambda e, ls=ln_s, qs=q_scr: e.tensor_tensor(ls[:, 3:4], ls[:, 3:4], ls[:, 4:5], OP.subtract))
            Vc(lambda e, ls=ln_s, qs=q_scr: e.tensor_scalar(ls[:, 3:4], ls[:, 3:4], EPS_LN, None, OP.add))
            P.wait_cur("scalar", "vch")
            Sc(lambda e, ls=ln_s, qs=q_scr: e.activation(ls[:, 5:6], ls[:, 3:4], ACT.Sqrt))
            P.wait_cur("vector", "sch")
            Vc(lambda e, ls=ln_s, qs=q_scr: e.reciprocal(ls[:, 6:7], ls[:, 5:6]))
            Vc(lambda e, ls=ln_s, qs=q_scr: e.tensor_tensor(ls[:, 4:5], ls[:, 6:7], ls[:, 6:7], OP.mult))
            Vc(lambda e, ls=ln_s, qs=q_scr: e.tensor_tensor(ls[:, 4:5], ls[:, 4:5], ls[:, 3:4], OP.mult))
            Vc(lambda e, ls=ln_s, qs=q_scr: e.tensor_scalar(ls[:, 4:5], ls[:, 4:5], -0.5, 1.5, OP.mult, OP.add))
            Vc(lambda e, ls=ln_s, qs=q_scr: e.tensor_tensor(ls[:, 6:7], ls[:, 6:7], ls[:, 4:5], OP.mult))
            Vc(lambda e, x2=x_t, ls=ln_s, qs=q_scr: e.tensor_scalar(xn_t[:], x2[:], ls[:, 2:3], ls[:, 6:7],
                                                 OP.subtract, OP.mult))
            if tt >= 2:
                P.wait("vector", "xpk", xpk_after[tt - 2])
            Vc(lambda e, d=xnb_t, ls=ln_s, qs=q_scr: e.tensor_copy(d[:], xn_t[:]))
            xnb_cnt = P.c("vch")
            Vc(lambda e, ls=ln_s, qs=q_scr: e.reduce_max(ls[:, 7:8], xn_t[:], axis=AX,
                                      apply_absolute_value=True))
            Vc(lambda e, ls=ln_s, qs=q_scr: e.tensor_scalar(ls[:, 7:8], ls[:, 7:8], EPS_Q, None, OP.max))
            Vc(lambda e, ls=ln_s, qs=q_scr: e.reciprocal(ls[:, 8:9], ls[:, 7:8]))
            Vc(lambda e, ls=ln_s, qs=q_scr: e.tensor_scalar(ls[:, 9:10], ls[:, 8:9], 128.0, None, OP.mult))
            Vc(lambda e, ls=ln_s, qs=q_scr: e.tensor_scalar(ls[:, 10:11], ls[:, 7:8], 1.0 / 128.0, None, OP.mult))
            P.wait_cur("scalar", "vch")
            P.wait_cur("scalar", "c_ld")
            Sc(lambda e, ls=ln_s, qs=q_scr: e.activation(qs[:], xn_t[:], ACT.Identity,
                                      bias=MAG_AP, scale=ls[:, 9:10]))
            P.wait_cur("vector", "sch")
            Vc(lambda e, ls=ln_s, qs=q_scr: e.tensor_scalar(qs[:], qs[:], -MAGIC, None, OP.add))
            Vc(lambda e, ls=ln_s, qs=q_scr: e.tensor_scalar(qs[:], qs[:], 127.0, -128.0, OP.min, OP.max))
            Vc(lambda e, d=xqa_t, ls=ln_s, qs=q_scr: e.tensor_scalar(d[:], qs[:], ls[:, 10:11], None, OP.mult))
            P.wait("gpsimd", "vch", P.c("vch"))
            P.dma("gpsimd", xqatok_park[tt * 128:(tt + 1) * 128, :], xqa_t[:], sem="xpk")
            for _ in range(2):
                if abs_pieces_idx[0] < len(abs_pieces):
                    emit_abs_piece(abs_pieces_idx[0])
                    abs_pieces_idx[0] += 1
            xnsb = xns0 if par == 0 else xns1
            if tt == 0:
                P.wait("tensor", "c_ld", P.c("c_ld"))
            P.wait("tensor", "vch", xnb_cnt)
            P.wait("tensor", "xte", P.c("xte"))
            for c in range(16):
                T(lambda e, cc2=c, xb=xnb_t: e.transpose(
                    ps_tb[:, cc2 * 128:(cc2 + 1) * 128],
                    xb[:, cc2 * 128:(cc2 + 1) * 128], c_identb[:]), sem="xtp", n=1)
            P.wait("scalar", "xtp", P.c("xtp"))
            P.wait("scalar", "xns_st", P.c("xns_st"))
            for c in range(16):
                S(lambda e, cc2=c, b=xnsb: e.copy(
                    b[:, cc2 * 128:(cc2 + 1) * 128],
                    ps_tb[:, cc2 * 128:(cc2 + 1) * 128]), sem="xte", n=1)
            P.wait("gpsimd", "xte", P.c("xte"))
            xdst = cc_xnT_inA if tt < 4 else cc_xnT_inB
            tcol = (tt % 4) * 128
            P.dma("gpsimd",
                  xdst.rearrange("(c p) t -> p c t", p=128)[:, :, tcol:tcol + 128],
                  xnsb[:].rearrange("p (c t) -> p c t", c=16), sem="xns_st")
            if tt == 3:
                P.wait("gpsimd", "xns_st", P.c("xns_st"))
                G(lambda e: e.collective_compute(
                    "AllGather", OP.bypass,
                    ins=[cc_xnT_inA[:]], outs=[cc_xnT_outA[:]],
                    replica_groups=[list(range(NCORE))]), sem="cc1", n=1)
            vch_after_x[tt] = P.c("vch")
            sch_after_x[tt] = P.c("sch")
            xpk_after[tt] = P.c("xpk")

        while abs_pieces_idx[0] < len(abs_pieces):
            emit_abs_piece(abs_pieces_idx[0])
            abs_pieces_idx[0] += 1
        P.wait("vector", "abss", P.c("abss"))
        Vc(lambda e: e.reduce_sum(abs_acc[:, 16:17], abs_acc[:, 0:4], axis=AX))
        Vc(lambda e: e.reduce_sum(abs_acc[:, 17:18], abs_acc[:, 4:8], axis=AX))
        Vc(lambda e: e.reduce_sum(abs_acc[:, 18:19], abs_acc[:, 8:16], axis=AX))
        Vc(lambda e: e.memset(abs_acc[:, 19:20], 0.0))
        P.wait_cur("gpsimd", "vch")
        P.dma("gpsimd", cc_absw_in[:, :], abs_acc[:, 16:20], sem="pk")

        P.wait("gpsimd", "pk", P.c("pk"))
        G(lambda e: e.collective_compute(
            "AllGather", OP.bypass,
            ins=[cc_absw_in[:]], outs=[cc_absw_out[:]],
            replica_groups=[list(range(NCORE))]), sem="cc1", n=1)
        P.wait("gpsimd", "xns_st", P.c("xns_st"))
        G(lambda e: e.collective_compute(
            "AllGather", OP.bypass,
            ins=[cc_xnT_inB[:]], outs=[cc_xnT_outB[:]],
            replica_groups=[list(range(NCORE))]), sem="cc1", n=1)
        P.barrier("bar_p0")

    ph12 = ExitStack()
    sb12 = lambda nm, shp, dt=F32: ph12.enter_context(nc.sbuf_tensor(nm, shp, dt))
    q_all = sb12("q_all", [128, HOWN * NTOK], BF16)
    k_all = sb12("k_all", [128, HOWN * NTOK], BF16)
    v_all = sb12("v_all", [128, 64 * VCW], BF16)
    wq_b = sb12("wq_b", [128, 16 * FOWN], BF16)
    wk_b = sb12("wk_b", [128, 16 * FOWN], BF16)
    wv_b = sb12("wv_b", [128, 16 * FOWN], BF16)
    wo_b = sb12("wo_b", [128, 2 * D], BF16)
    twscr = sb12("twscr", [128, 2048])
    tgb = sb12("tgb", [128, 2048], BF16)
    ttb = sb12("ttb", [128, 2048], BF16)

    with ExitStack() as ph1:
        sb1 = lambda nm, shp, dt=F32: ph1.enter_context(nc.sbuf_tensor(nm, shp, dt))
        xstrip = sb1("xstrip", [128, 16 * 512], BF16)
        xstrip2 = sb1("xstrip2", [128, 16 * 512], BF16)
        cs_t = sb1("cs_t", [64, 2 * 512])
        cs_t2 = sb1("cs_t2", [64, 2 * 512])
        rp_scr = sb1("rp_scr", [64, 4 * 512])

        for wsrc, wdst in ((wqT_s, wq_b), (wkT_s, wk_b), (wvT_s, wv_b)):
            for half in range(2):
                P.wait("sync", "wcast", P.c("wcast"))
                P.dma("sync", twscr[:, 0:2048].rearrange("p (c f) -> p c f", c=8),
                      wsrc[half * 1024:(half + 1) * 1024, :].rearrange(
                          "(c p) f -> p c f", p=128), sem="wld")
                P.wait("vector", "wld", P.c("wld"))
                V(lambda e, d=wdst, h=half: e.tensor_copy(
                    d[:, h * 2048:(h + 1) * 2048], twscr[:, 0:2048]), sem="wcast", n=1)
        for c in range(2):
            P.wait("sync", "wcast", P.c("wcast"))
            P.dma("sync", twscr[:, 0:2048], woT_s[c * 128:(c + 1) * 128, :], sem="wld")
            P.wait("vector", "wld", P.c("wld"))
            V(lambda e, cc=c: e.tensor_copy(wo_b[:, cc * 2048:(cc + 1) * 2048],
                                            twscr[:, 0:2048]), sem="wcast", n=1)

        for g in range(2):
            Vc(lambda e, gg=g: e.memset(
                v_all[:].rearrange("p (c g e) -> p c g e", c=64, g=2)[:, :, gg, HD:HD + 1],
                1.0))

        P.wait("sync", "cc1", 2)
        P.dma("sync", abs_acc[:, 0:NCORE * 4].rearrange("p (r m) -> p r m", r=NCORE),
              cc_absw_out[:].rearrange("r p m -> p r m"), sem="aw_ld")
        P.wait("vector", "aw_ld", P.c("aw_ld"))
        Vc(lambda e: e.reduce_sum(
            abs_acc[:, 16:20],
            abs_acc[:, 0:NCORE * 4].rearrange("p (r m) -> p m r", r=NCORE), axis=AX))
        P.wait_cur("tensor", "vch")
        P.wait("tensor", "xte", P.c("xte"))
        T(lambda e: e.matmul(ps_o[:, 0:4], c_onesf[:], abs_acc[:, 16:20],
                             start=True, stop=True), sem="mm_mean", n=1)
        P.wait("vector", "mm_mean", P.c("mm_mean"))
        Vc(lambda e: e.memset(invN[:, 0:3], 1.0 / (D * NI)))
        Vc(lambda e: e.memset(invN[:, 3:4], 1.0))
        Vc(lambda e: e.tensor_copy(sc_mean[:, :], ps_o[:, 0:4]))
        Vc(lambda e: e.tensor_tensor(sc_mean[:, :], sc_mean[:, :], invN[:, :], OP.mult))
        Vc(lambda e: e.tensor_scalar(sc_mean[:, :], sc_mean[:, :], EPS_Q, None, OP.max))
        Vc(lambda e: e.reciprocal(sc_sw[:, 0:1], sc_mean[:, 0:1]))
        Vc(lambda e: e.reciprocal(sc_sw[:, 1:2], sc_mean[:, 1:2]))
        Vc(lambda e: e.reciprocal(sc_sw[:, 2:3], sc_mean[:, 2:3]))
        if "mean" in taps:
            P.wait_cur("gpsimd", "vch")
            P.dma("gpsimd", taps["mean"], sc_mean[:, :], sem="tapd")

        P.wait("tensor", "wcast", P.c("wcast"))
        qkv_mm_after = {}
        rpo_w_after = {}
        rpo_after = {}
        vev_par = {0: 0, 1: 0}
        for w in range(NTOK // 512):
            r, off = w // 2, (w % 2) * 512
            xs = xstrip if (w % 2 == 0) else xstrip2
            cs = cs_t if (w % 2 == 0) else cs_t2
            lsem = "xs_ld0" if (w % 2 == 0) else "xs_ld1"
            if w >= 2:
                P.wait("sync", "qkv_mm", qkv_mm_after[w - 2])
                P.wait("sync", "rpo", rpo_w_after[w - 2])
            P.wait("sync", "cc1", 1 if off == 0 else 3)
            xsrc = cc_xnT_outA if off == 0 else cc_xnT_outB
            P.dma("sync", xs[:].rearrange("p (c t) -> p c t", c=16),
                  xsrc[r, :, 0:512].rearrange("(c p) t -> p c t", p=128),
                  sem=lsem)
            P.dma("sync", cs[:, 0:512], cosT[:, w * 512:(w + 1) * 512], sem=lsem)
            P.dma("sync", cs[:, 512:1024], sinT[:, w * 512:(w + 1) * 512], sem=lsem)
            P.wait("tensor", lsem, P.c(lsem))
            P.wait("vector", lsem, P.c(lsem))
            for mat, wbuf, dstall, pp in ((0, wq_b, q_all, ps_a), (1, wk_b, k_all, ps_b)):
                for f2 in range(HOWN):
                    pdst = pp[:, f2 * 512:(f2 + 1) * 512]
                    P.wait("tensor", "rpo", rpo_after.get((w - 1, mat, f2)))
                    for fi in range(16):
                        T(lambda e, p=pdst, wb=wbuf, f=fi, ff=f2, x2=xs:
                          e.matmul(p, wb[:, f * FOWN + ff * 128:f * FOWN + ff * 128 + 128],
                                   x2[:, f * 512:(f + 1) * 512],
                                   start=(f == 0), stop=(f == 15)), sem="qkv_mm", n=1)
                    P.wait("vector", "qkv_mm", P.c("qkv_mm"))
                    P.wait("vector", "rpo", P.c("rpo"))
                    for i4, (rows, cx) in enumerate((((0, 64), (0, 512)),
                                                     ((64, 128), (512, 1024)),
                                                     ((0, 64), (512, 1024)),
                                                     ((64, 128), (0, 512)))):
                        V(lambda e, p=pdst, rw=rows, cc2=cx, i=i4, c3=cs: e.tensor_tensor(
                            rp_scr[:, i * 512:(i + 1) * 512],
                            p[rw[0]:rw[1], :], c3[:, cc2[0]:cc2[1]], OP.mult),
                          sem="rp", n=1)
                    dcol = f2 * NTOK + w * 512
                    P.wait_cur("vector", "rp")
                    V(lambda e, d=dstall, c2=dcol: e.tensor_tensor(
                        d[0:64, c2:c2 + 512], rp_scr[:, 0:512], rp_scr[:, 512:1024],
                        OP.subtract), sem="rpo", n=1)
                    V(lambda e, d=dstall, c2=dcol: e.tensor_tensor(
                        d[64:128, c2:c2 + 512], rp_scr[:, 1024:1536],
                        rp_scr[:, 1536:2048], OP.add), sem="rpo", n=1)
                    rpo_after[(w, mat, f2)] = P.c("rpo")
            for ts in range(4):
                par = ts % 2
                pv = ps_c[:, par * 512:par * 512 + 256]
                P.wait("tensor", "vev", vev_par[par])
                for fi in range(16):
                    T(lambda e, p=pv, f=fi, t=ts, x2=xs:
                      e.matmul(p, x2[:, f * 512 + t * 128:f * 512 + t * 128 + 128],
                               wv_b[:, f * FOWN:(f + 1) * FOWN],
                               start=(f == 0), stop=(f == 15)), sem="qkv_mm", n=1)
                P.wait("vector", "qkv_mm", P.c("qkv_mm"))
                P.wait("vector", "vch", P.c("vch"))
                tc = w * 4 + ts
                V(lambda e, p=pv, c2=tc: e.tensor_copy(
                    v_all[:, c2 * VCW:c2 * VCW + HD], p[:, 0:HD]), sem="vev", n=1)
                V(lambda e, p=pv, c2=tc: e.tensor_copy(
                    v_all[:, c2 * VCW + HD + 1:c2 * VCW + 2 * HD + 1],
                    p[:, HD:2 * HD]), sem="vev", n=1)
                vev_par[par] = P.c("vev")
            rpo_w_after[w] = P.c("rpo")
            qkv_mm_after[w] = P.c("qkv_mm")
        P.barrier("bar_qkv")

        if "q_all" in taps:
            for h in range(HOWN):
                P.dma("gpsimd", taps["q_all"][h], q_all[:, h * NTOK:(h + 1) * NTOK],
                      sem="tapd")
        if "k_all" in taps:
            for h in range(HOWN):
                P.dma("gpsimd", taps["k_all"][h], k_all[:, h * NTOK:(h + 1) * NTOK],
                      sem="tapd")
        if "v_all" in taps:
            P.dma("gpsimd", taps["v_all"], v_all[:, :], sem="tapd")
        if taps:
            P.wait("gpsimd", "tapd", P.c("tapd"))
            P.barrier("bar_tap1")

    tern_pieces = []
    tern_done = [0]
    for (src_dram, dst_dram, m_idx) in ((gT_s, cc_tg_in, 0), (uT_s, cc_tu_in, 1),
                                        (dT_s, cc_td_in, 2)):
        ntile = src_dram.shape[0] // 128
        for it in range(ntile):
            for piece in range(src_dram.shape[1] // 2048):
                tern_pieces.append((src_dram, dst_dram, m_idx, it, piece))

    def emit_tern_piece(spec):
        src_dram, dst_dram, m_idx, it, piece = spec
        P.wait("sync", "terv", P.c("terv"))
        P.dma("sync", twscr[:, 0:2048],
              src_dram[it * 128:(it + 1) * 128,
                       piece * 2048:(piece + 1) * 2048], sem="ter_ld")
        P.wait("vector", "ter_ld", P.c("ter_ld"))
        P.wait("vector", "ter_st", P.c("ter_st"))
        V(lambda e, m=m_idx: e.tensor_scalar(
            tgb[:, 0:2048], twscr[:, 0:2048], sc_sw[:, m:m + 1], 0.5,
            OP.mult, OP.is_ge), sem="terv", n=1)
        V(lambda e, m=m_idx: e.tensor_scalar(
            ttb[:, 0:2048], twscr[:, 0:2048], sc_sw[:, m:m + 1], -0.5,
            OP.mult, OP.is_le), sem="terv", n=1)
        P.wait_cur("vector", "terv")
        V(lambda e: e.tensor_tensor(
            ttb[:, 0:2048], tgb[:, 0:2048], ttb[:, 0:2048], OP.subtract),
          sem="terv", n=1)
        P.wait("gpsimd", "terv", P.c("terv"))
        P.dma("gpsimd", dst_dram[it * 128:(it + 1) * 128,
                                 piece * 2048:(piece + 1) * 2048],
              ttb[:, 0:2048], sem="ter_st")

    def emit_cc2_one(i):
        tin, tout = ((cc_tg_in, cc_tg_out), (cc_tu_in, cc_tu_out),
                     (cc_td_in, cc_td_out))[i]
        P.wait("gpsimd", "ter_st", P.c("ter_st"))
        G(lambda e, a=tin, o=tout: e.collective_compute(
            "AllGather", OP.bypass,
            ins=[a[:]], outs=[o[:]],
            replica_groups=[list(range(NCORE))]), sem="cc2", n=1)

    with ExitStack() as ph2:
        sb2 = lambda nm, shp, dt=F32: ph2.enter_context(nc.sbuf_tensor(nm, shp, dt))
        s2_sb = sb2("s2_sb", [128, 4 * D], BF16)
        s4_sb = sb2("s4_sb", [128, 4 * D], BF16)
        att_q = sb2("att_q", [128, 4 * FOWN], BF16)
        attT = sb2("attT", [128, 2 * QB], BF16)
        oep = sb2("oep", [128, 16])
        wo_ev = sb2("wo_ev", [128, 4 * D], BF16)

        p1_after = {}
        p2_after = {}
        av_after = {}
        atp_after = 0
        p1_all = p2_all = av_all = 0
        qb_order = list(range(0, NQB, 2)) + list(range(1, NQB, 2))
        units = [(qi, q, hh) for qi, q in enumerate(qb_order) for hh in range(HOWN)]
        p1_after = {}
        p2_after = {}
        p2_half = {}
        av_after = {}
        gctr = [0]
        tails = [None]

        def emit_scores(j, qb, h, g):
            qcol = h * NTOK + qb * 512
            slot = g % 4
            s2o = slot * D
            for g2, sps in ((0, ps_a), (1, ps_b)):
                if g2 == 0:
                    P.wait("scalar", "p2", p2_after.get(g - 4, 0))
                    P.wait("vector", "avmm", av_after.get(g - 4, 0))
                for kc in (2 * g2, 2 * g2 + 1):
                    P.wait("tensor", "p1",
                           max(0, p1_after.get(g - 1, 0) - (3 - kc)))
                    last = (kc == 2 * g2 + 1)
                    T(lambda e, p=sps, k=kc, jj=j, hh=h, qc=qcol:
                      e.matmul(p[:, (k % 2) * 512:(k % 2) * 512 + 512],
                               k_all[:, hh * NTOK + jj * 512 + k * 128:
                                     hh * NTOK + jj * 512 + k * 128 + 128],
                               q_all[:, qc:qc + 512],
                               start=True, stop=True),
                      sem="smm", n=1)
                for kc in (2 * g2, 2 * g2 + 1):
                    P.wait("scalar", "smm", P.c("smm") - (1 if kc == 2 * g2 else 0))
                    S(lambda e, p=sps, k=kc, o=s2o: e.activation(
                        s2_sb[:, o + k * 512:o + k * 512 + 512],
                        p[:, (k % 2) * 512:(k % 2) * 512 + 512],
                        ACT.Square), sem="p1", n=1)
            p1_after[g] = P.c("p1")
            P.wait("vector", "p1", max(0, P.c("p1") - 2))
            if j == qb:
                V(lambda e, o=s2o: e.tensor_tensor(
                    s2_sb[:, o:o + 1024], s2_sb[:, o:o + 1024],
                    c_mask[:, 0:1024], OP.mult), sem="p2", n=1)
                P.wait_cur("vector", "p2")
            V(lambda e, o=s2o: e.tensor_tensor(
                s4_sb[:, o:o + 1024], s2_sb[:, o:o + 1024], s2_sb[:, o:o + 1024],
                OP.mult), sem="p2", n=1)
            p2_half[g] = P.c("p2")
            P.wait("vector", "p1", P.c("p1"))
            if j == qb:
                V(lambda e, o=s2o: e.tensor_tensor(
                    s2_sb[:, o + 1024:o + D], s2_sb[:, o + 1024:o + D],
                    c_mask[:, 1024:2048], OP.mult), sem="p2", n=1)
                P.wait_cur("vector", "p2")
            V(lambda e, o=s2o: e.tensor_tensor(
                s4_sb[:, o + 1024:o + D], s2_sb[:, o + 1024:o + D],
                s2_sb[:, o + 1024:o + D], OP.mult), sem="p2", n=1)
            p2_after[g] = P.c("p2")

        def emit_av(j, qb, h, g, first):
            slot = g % 4
            s2o = slot * D
            P.wait("tensor", "p2", p2_half[g])
            if first:
                P.wait("tensor", "vch", P.c("vch"))
                P.wait("tensor", "atpe", P.c("atpe"))
                P.wait("tensor", "woev", P.c("woev"))
            diag = (j == qb)
            nmm = 10 if diag else 16
            for kc in range(4):
                if kc == 2:
                    P.wait("tensor", "p2", p2_after[g])
                vbase = (j * 4 + kc) * VCW + h * (HD + 1)
                for k in range(4):
                    if diag and kc > k:
                        continue
                    last = (kc == 3 and k == 3)
                    stop = (j == qb and kc == min(3, k))
                    T(lambda e, kk=k, kcc=kc, o=s2o, vb=vbase, jj=j, st=stop:
                      e.matmul((ps_o if kk < 2 else ps_c)[:, (kk % 2) * 512:
                                                          (kk % 2) * 512 + 129],
                               s4_sb[:, o + kcc * 512 + kk * 128:
                                     o + kcc * 512 + kk * 128 + 128],
                               v_all[:, vb:vb + 129],
                               start=(jj == 0 and kcc == 0),
                               stop=st),
                      sem=("avmm" if last else None), n=nmm)
            av_after[g] = P.c("avmm")

        def mk_fin(qb, h, qbi):
            def fin():
                P.wait("vector", "avmm", P.c("avmm"))
                P.wait("vector", "atp", P.c("atp"))
                for k in range(4):
                    pav = (ps_o if k < 2 else ps_c)[:, (k % 2) * 512:(k % 2) * 512 + 129]
                    Vc(lambda e, p=pav: e.tensor_scalar(
                        oep[:, 0:1], p[:, 128:129], EPS_ATT, None, OP.max))
                    Vc(lambda e: e.reciprocal(oep[:, 8:9], oep[:, 0:1]))
                    Vc(lambda e, p=pav, kk=k, hh=h: e.tensor_scalar(
                        att_q[:, kk * FOWN + hh * HD:kk * FOWN + hh * HD + HD],
                        p[:, 0:128], oep[:, 8:9], None, OP.mult))
                P.wait("tensor", "vch", P.c("vch"))
                if tern_pieces:
                    emit_tern_piece(tern_pieces.pop(0))
                    tern_done[0] += 1
                    if tern_done[0] % 8 == 0:
                        emit_cc2_one(tern_done[0] // 8 - 1)
                if h != HOWN - 1:
                    return
                P.wait("vector", "womm", P.c("womm"))
                for fc in range(2):
                    for tc2 in range(4):
                        slot = (tc2 % 2) * 1024
                        if P.c("atpe") >= 2:
                            P.wait("tensor", "atpe", P.c("atpe") - 1)
                        T(lambda e, f=fc, t=tc2, sl=slot: e.transpose(
                            ps_tb[:, sl:sl + 128],
                            att_q[:, t * FOWN + f * 128:t * FOWN + f * 128 + 128],
                            c_identb[:]), sem="atp", n=1)
                        P.wait("vector", "atp", P.c("atp"))
                        V(lambda e, f=fc, t=tc2, sl=slot: e.tensor_copy(
                            attT[:, f * QB + t * 128:f * QB + t * 128 + 128],
                            ps_tb[:, sl:sl + 128]), sem="atpe", n=1)
                P.wait("tensor", "atpe", P.c("atpe"))
                for tc2 in range(4):
                    wop = tc2 * D
                    P.wait("tensor", "woev", P.c("woev"))
                    for fc in range(2):
                        for dt4 in range(4):
                            pww = (ps_c if dt4 < 2 else ps_o)[:, (dt4 % 2) * 512:(dt4 % 2) * 512 + 512]
                            T(lambda e, p=pww, f=fc, t=tc2, d=dt4:
                              e.matmul(p, attT[:, f * QB + t * 128:f * QB + t * 128 + 128],
                                       wo_b[:, f * D + d * 512:f * D + d * 512 + 512],
                                       start=(f == 0), stop=(f == 1)), sem="womm", n=1)
                    P.wait("scalar", "womm", P.c("womm"))
                    if P.c("wost") >= 64:
                        P.wait("scalar", "wost", P.c("wost") - 48)
                    for dt4 in range(4):
                        pww = (ps_c if dt4 < 2 else ps_o)[:, (dt4 % 2) * 512:(dt4 % 2) * 512 + 512]
                        S(lambda e, p=pww, d=dt4, w2=wop: e.activation(
                            wo_ev[:, w2 + d * 512:w2 + d * 512 + 512], p,
                            ACT.Identity), sem="woev", n=1)
                    P.wait("gpsimd", "woev", P.c("woev"))
                    dst_half = att_partA if (qb % 2 == 0) else att_partB
                    drow = (qb // 2) * 512 + tc2 * 128
                    P.dma("gpsimd", dst_half[drow:drow + 128, :],
                          wo_ev[:, wop:wop + D], sem="wost")
                if qbi == NQB // 2 - 1:
                    P.wait("gpsimd", "wost", P.c("wost"))
                    G(lambda e: e.collective_compute(
                        "ReduceScatter", OP.add,
                        ins=[att_partA[:]], outs=[rs_outA[:]],
                        replica_groups=[list(range(NCORE))]), sem="cc3", n=1)
            return fin

        DEFER = False
        for qbi, qb, h in units:
            g0 = gctr[0]
            emit_scores(0, qb, h, g0)
            t3 = tails[0]
            if t3 is not None and DEFER:
                t3[0]()
            if qb >= 1:
                emit_scores(1, qb, h, g0 + 1)
            if t3 is not None and DEFER:
                t3[1]()
                t3[2]()
                tails[0] = None
            for j in range(2, qb + 1):
                emit_av(j - 2, qb, h, g0 + j - 2, first=(j == 2))
                emit_scores(j, qb, h, g0 + j)
            gctr[0] += qb + 1
            if qb == 0:
                tails[0] = [lambda: None, lambda: None, None]
                tails[0][1] = (lambda q2=qb, h2=h, gg=g0: emit_av(0, q2, h2, gg, True))
                tails[0][2] = mk_fin(qb, h, qbi)
            else:
                first1 = (qb == 1)
                tails[0] = [
                    (lambda q2=qb, h2=h, gg=g0, f1=first1:
                     emit_av(q2 - 1, q2, h2, gg + q2 - 1, f1)),
                    (lambda q2=qb, h2=h, gg=g0: emit_av(q2, q2, h2, gg + q2, False)),
                    mk_fin(qb, h, qbi)]
            if not DEFER:
                t3 = tails[0]
                t3[0]()
                t3[1]()
                t3[2]()
                tails[0] = None
        if tails[0] is not None:
            t3 = tails[0]
            t3[0]()
            t3[1]()
            t3[2]()
            tails[0] = None

        P.wait("gpsimd", "wost", P.c("wost"))
        G(lambda e: e.collective_compute(
            "ReduceScatter", OP.add,
            ins=[att_partB[:]], outs=[rs_outB[:]],
            replica_groups=[list(range(NCORE))]), sem="cc3", n=1)
        P.barrier("bar_att")

    ph12.close()

    ph3y = ExitStack()
    yA = ph3y.enter_context(nc.sbuf_tensor("yA", [128, 64 * TW], BF16))
    with ExitStack() as ph3:
        sb3 = lambda nm, shp, dt=F32: ph3.enter_context(nc.sbuf_tensor(nm, shp, dt))
        xqaT = sb3("xqaT", [128, 16 * TOWN], BF16)
        gstrip = sb3("gstrip", [128, 16 * 128], BF16)
        gstrip2 = sb3("gstrip2", [128, 16 * 128], BF16)
        ustrip = sb3("ustrip", [128, 16 * 128], BF16)
        ustrip2 = sb3("ustrip2", [128, 16 * 128], BF16)
        sg_sb = sb3("sg_sb", [128, 2 * TOWN], BF16)
        yB = sb3("yB", [128, 64 * TW], BF16)
        bfs0 = sb3("bfs0a", [128, TW], BF16)
        bfs1 = sb3("bfs1a", [128, TW], BF16)

        bf6A = {}

        def emit_bfly_group(st, g0):
            dlt = 1 << (st - 1)
            P.wait("vector", "ypkv", P.c("ypkv"))
            for yb in (yA, yB):
                for off in range(dlt):
                    ai, bi = g0 + off, g0 + off + dlt
                    tb6 = bfs0 if (off % 2 == 0) else bfs1
                    P.wait("vector", "bf", max(0, P.c("bf") - 5))
                    V(lambda e, a=ai, b2=bi, tb=tb6, y=yb: e.tensor_tensor(
                        tb[:], y[:, a * TW:(a + 1) * TW],
                        y[:, b2 * TW:(b2 + 1) * TW], OP.add), sem="bf", n=1)
                    P.wait_cur("vector", "bf")
                    V(lambda e, a=ai, b2=bi, y=yb: e.tensor_tensor(
                        y[:, b2 * TW:(b2 + 1) * TW],
                        y[:, a * TW:(a + 1) * TW],
                        y[:, b2 * TW:(b2 + 1) * TW], OP.subtract), sem="bf", n=1)
                    P.wait_cur("vector", "bf")
                    V(lambda e, a=ai, tb=tb6, y=yb: e.tensor_copy(
                        y[:, a * TW:(a + 1) * TW], tb[:]), sem="bf", n=1)
                    if st == 6 and yb is yA:
                        bf6A[off] = P.c("bf")

        P.wait("sync", "xpk", P.c("xpk"))
        for c in range(16):
            P.op("sync", lambda e, cc2=c: e.dma_start(
                out=xqaT[:, cc2 * 1024:(cc2 + 1) * 1024],
                in_=xqatok_park[:, cc2 * 128:(cc2 + 1) * 128],
                transpose=True), sem="xq_ld", n=16)
        P.wait("tensor", "xq_ld", P.c("xq_ld"))
        if "xqaT" in taps:
            P.wait("gpsimd", "xq_ld", P.c("xq_ld"))
            P.dma("gpsimd", taps["xqaT"], xqaT[:, :], sem="tapd")
            P.wait("tensor", "tapd", P.c("tapd"))

        gu_after = {}
        sg_after = {}
        y_after = {}
        sgypk_after = {}
        for b in range(64):
            par = b % 2
            gs = gstrip if par == 0 else gstrip2
            us = ustrip if par == 0 else ustrip2
            psg = ps_a if par == 0 else ps_o
            psu = ps_b if par == 0 else ps_c
            sgb = sg_sb[:, 0:TOWN] if par == 0 else sg_sb[:, TOWN:2 * TOWN]
            lsem = "gu_ld0" if par == 0 else "gu_ld1"
            if b >= 2:
                P.wait("sync", "gu_mm", gu_after[b - 2])
            if b == 0:
                P.wait("sync", "cc2", 1)
            P.dma("sync", gs[:].rearrange("p (c i) -> p c i", c=16),
                  cc_tg_out[:, b * 128:(b + 1) * 128].rearrange(
                      "(c p) i -> p c i", p=128), sem=lsem)
            if b == 0:
                P.wait("sync", "cc2", 2)
            P.dma("sync", us[:].rearrange("p (c i) -> p c i", c=16),
                  cc_tu_out[:, b * 128:(b + 1) * 128].rearrange(
                      "(c p) i -> p c i", p=128), sem=lsem)
            P.wait("tensor", lsem, P.c(lsem))
            if b >= 2:
                P.wait("tensor", "sgs", sg_after[b - 2])
                P.wait("tensor", "ypkv", y_after[b - 2])
            for tt2 in range(2):
                for hc in range(16):
                    T(lambda e, g2=gs, h2=hc, t=tt2, p=psg:
                      e.matmul(p[:, t * 512:(t + 1) * 512],
                               g2[:, h2 * 128:(h2 + 1) * 128],
                               xqaT[:, h2 * TOWN + t * 512:h2 * TOWN + t * 512 + 512],
                               start=(h2 == 0), stop=(h2 == 15)), sem="gu_mm", n=1)
            for tt2 in range(2):
                for hc in range(16):
                    T(lambda e, u2=us, h2=hc, t=tt2, p=psu:
                      e.matmul(p[:, t * 512:(t + 1) * 512],
                               u2[:, h2 * 128:(h2 + 1) * 128],
                               xqaT[:, h2 * TOWN + t * 512:h2 * TOWN + t * 512 + 512],
                               start=(h2 == 0), stop=(h2 == 15)), sem="gu_mm", n=1)
            gu_after[b] = P.c("gu_mm")
            P.wait("scalar", "gu_mm", P.c("gu_mm"))
            if b >= 2:
                P.wait("scalar", "ypkv", sgypk_after[b - 2])
            S(lambda e, p=psg, s2=sgb: e.activation(
                s2, p[:, 0:1024], ACT.Silu,
                bias=ZERO_AP, scale=sc_mean[:, 0:1]), sem="sgs", n=1)
            sg_after[b] = P.c("sgs")
            P.wait("vector", "sgs", P.c("sgs"))
            V(lambda e, p=psu, s2=sgb, bb=b: e.scalar_tensor_tensor(
                yA[:, bb * TW:(bb + 1) * TW], p[:, 0:512],
                sc_mean[:, 1:2], s2[0:128, 0:512], OP.mult, OP.mult), sem="ypkv", n=1)
            V(lambda e, p=psu, s2=sgb, bb=b: e.scalar_tensor_tensor(
                yB[:, bb * TW:(bb + 1) * TW], p[:, 512:1024],
                sc_mean[:, 1:2], s2[0:128, 512:1024], OP.mult, OP.mult), sem="ypkv", n=1)
            y_after[b] = P.c("ypkv")
            sgypk_after[b] = P.c("ypkv")
            for st6 in range(1, 7):
                if (b + 1) % (1 << st6) == 0:
                    emit_bfly_group(st6, b + 1 - (1 << st6))
        P.wait("gpsimd", "bf", P.c("bf"))
        P.dma("gpsimd", ypark1[:].rearrange("c p t -> p c t"),
              yB[:].rearrange("p (c t) -> p c t", c=64), sem="ypk")
        for e in ENGS:
            P.wait(e, "ypk", P.c("ypk"))

    with ExitStack() as ph3b:
        sb3b = lambda nm, shp, dt=F32: ph3b.enter_context(nc.sbuf_tensor(nm, shp, dt))
        yhalf = yA
        zscr = sb3b("zscr", [128, NI], BF16)
        zq_T = sb3b("zq_T", [128, 64 * TW], BF16)
        zst = sb3b("zst", [128, 32])
        zmax = sb3b("zmax", [128, 64])
        gbc = sb3b("gbc", [128, 2048])
        zqe = sb3b("zqe", [128, 4 * 512])
        ffd_e = sb3b("ffd_e", [128, 512])
        dstrip = sb3b("dstrip", [128, 64 * 128], BF16)
        dstrip2 = zscr

        dn_after = {}
        dne_after = {}
        zq1_after = {}
        zq2_after = {}
        zq3_after = {}

        def load_y1():
            P.wait("sync", "zfmm", P.c("zfmm"))
            P.wait("sync", "ztmm", P.c("ztmm"))
            P.dma("sync", yhalf[:].rearrange("p (c t) -> p c t", c=64),
                  ypark1[:].rearrange("c p t -> p c t"), sem="yh_ld")

        def mk_stats_steps(half):
            zoff = half * 16
            goff = half * 1024
            steps = []

            def tc3_chunk(tc3):
                P.wait("tensor", "yh_ld", P.c("yh_ld"))
                chase = False
                if not chase:
                    P.wait("tensor", "bf", P.c("bf"))
                P.wait("tensor", "gbv", P.c("gbv"))
                border = ([x for k in range(32) for x in (k, k + 32)]
                          if chase else list(range(64)))
                for i, b in enumerate(border):
                    sl = i % 4
                    pz = (ps_a if sl < 2 else ps_o)[:, (sl % 2) * 512:(sl % 2) * 512 + 128]
                    if chase:
                        P.wait("tensor", "bf", bf6A[b % 32])
                    if P.c("zmx") >= 4:
                        P.wait("tensor", "zmx", P.c("zmx") - 3)
                    T(lambda e, bb=b, t=tc3, p=pz: e.matmul(
                        p, yhalf[:, bb * TW + t * 128:bb * TW + t * 128 + 128],
                        c_H128[:], start=True, stop=True), sem="ztmm", n=1)
                    P.wait("vector", "ztmm", P.c("ztmm"))
                    V(lambda e, bb=b, p=pz: e.reduce_max(
                        zmax[:, bb:bb + 1], p, axis=AX,
                        apply_absolute_value=True), sem="zmx", n=1)
                P.wait("vector", "zmx", P.c("zmx"))
                V(lambda e, t=tc3: e.reduce_max(
                    zst[:, zoff + t:zoff + t + 1], zmax[:, :], axis=AX),
                  sem="zs", n=1)

            for t in range(4):
                steps.append(lambda tt3=t: tc3_chunk(tt3))

            def stat_combine():
                P.wait("vector", "zs", P.c("zs"))
                P.wait("vector", "grt", P.c("grt"))
                Vc(lambda e: e.tensor_scalar(
                    zst[:, zoff + 4:zoff + 8], zst[:, zoff:zoff + 4], C_FWHT,
                    EPS_Q, OP.mult, OP.max))
                for i in range(4):
                    Vc(lambda e, ii=i: e.reciprocal(
                        zst[:, zoff + 8 + ii:zoff + 9 + ii],
                        zst[:, zoff + 4 + ii:zoff + 5 + ii]))
                Vc(lambda e: e.tensor_scalar(
                    zst[:, zoff + 8:zoff + 12], zst[:, zoff + 8:zoff + 12],
                    128.0 * C_FWHT, None, OP.mult))
                Vc(lambda e: e.tensor_scalar(
                    zst[:, zoff + 12:zoff + 16], zst[:, zoff + 4:zoff + 8],
                    sc_mean[:, 2:3], None, OP.mult))
                Vc(lambda e: e.tensor_scalar(
                    zst[:, zoff + 12:zoff + 16], zst[:, zoff + 12:zoff + 16],
                    1.0 / 128.0, None, OP.mult))
                P.wait("gpsimd", "vch", P.c("vch"))
                P.wait("gpsimd", "gbc_ld", P.c("gbc_ld"))
                P.dma("gpsimd", gam_rt[0, :].rearrange("(c p) -> p c", p=128),
                      zst[:, zoff + 8:zoff + 12], sem="grt", slow=True)
                P.dma("gpsimd", gam_rt[1, :].rearrange("(c p) -> p c", p=128),
                      zst[:, zoff + 12:zoff + 16], sem="grt", slow=True)
            steps.append(stat_combine)

            def gbc_mm():
                P.wait("sync", "grt", P.c("grt"))
                P.dma("sync", gbc[0:1, goff:goff + 512], gam_rt[0:1, :],
                      sem="gbc_ld")
                P.dma("sync", gbc[0:1, goff + 512:goff + 1024], gam_rt[1:2, :],
                      sem="gbc_ld")
                P.wait("tensor", "gbc_ld", P.c("gbc_ld"))
                P.wait("tensor", "gbv", P.c("gbv"))
                P.wait("tensor", "zmx", P.c("zmx"))
                T(lambda e: e.matmul(ps_o[:, 0:512], c_onesf[0:1, :],
                                     gbc[0:1, goff:goff + 512],
                                     start=True, stop=True), sem="gbmm", n=1)
                T(lambda e: e.matmul(ps_o[:, 512:1024], c_onesf[0:1, :],
                                     gbc[0:1, goff + 512:goff + 1024],
                                     start=True, stop=True), sem="gbmm", n=1)
                P.wait("vector", "gbmm", P.c("gbmm"))
                V(lambda e: e.tensor_copy(gbc[:, goff:goff + 512],
                                          ps_o[:, 0:512]), sem="gbv", n=1)
                V(lambda e: e.tensor_copy(gbc[:, goff + 512:goff + 1024],
                                          ps_o[:, 512:1024]), sem="gbv", n=1)
            steps.append(gbc_mm)
            return steps

        def sq_quant(half):
            goff = half * 1024
            P.wait("vector", "gbv", P.c("gbv"))
            P.wait("vector", "dnmm", P.c("dnmm"))
            P.wait("tensor", "dne", P.c("dne"))
            P.wait("tensor", "zq1", P.c("zq1"))
            P.wait("tensor", "zmx", P.c("zmx"))
            for b in range(64):
                sl = b % 4
                pz = (ps_a if sl < 2 else ps_b)[:, (sl % 2) * 512:(sl % 2) * 512 + 512]
                qa = zqe[:, (b % 2) * 1024:(b % 2) * 1024 + 512]
                qb2 = zqe[:, (b % 2) * 1024 + 512:(b % 2) * 1024 + 1024]
                if b >= 4:
                    P.wait("tensor", "zq1", zq1_after[b - 4])
                T(lambda e, bb=b, p=pz: e.matmul(
                    p, c_H128[:], yhalf[:, bb * TW:(bb + 1) * TW],
                    start=True, stop=True), sem="zfmm", n=1)
                P.wait("vector", "zfmm", P.c("zfmm"))
                if b >= 2:
                    P.wait("vector", "zq2", zq2_after[b - 2])
                V(lambda e, p=pz, q=qa: e.scalar_tensor_tensor(
                    q, p, 1.0, gbc[:, goff:goff + 512], OP.mult, OP.mult),
                  sem="zq1", n=1)
                zq1_after[b] = P.c("zq1")
                P.wait("vector", "zq1", P.c("zq1"))
                if b >= 2:
                    P.wait("vector", "zq3", zq3_after[b - 2])
                V(lambda e, q=qa, q2=qb2: e.tensor_scalar(
                    q2, q, MAGIC, -MAGIC, OP.add, OP.add), sem="zq2", n=1)
                zq2_after[b] = P.c("zq2")
                P.wait("vector", "zq2", P.c("zq2"))
                V(lambda e, bb=b, q2=qb2: e.tensor_scalar(
                    zq_T[:, bb * TW:(bb + 1) * TW], q2, 127.0, -128.0,
                    OP.min, OP.max), sem="zq3", n=1)
                zq3_after[b] = P.c("zq3")
            P.wait("tensor", "zq3", P.c("zq3"))

        def down(half, fill=()):
            goff = half * 1024
            fill = list(fill)
            fidx = [0]
            for dc in range(16):
                par = dc % 2
                ds = dstrip if par == 0 else dstrip2
                dsem = "dn_ld0" if par == 0 else "dn_ld1"
                prev = dn_after.get((half, dc - 2), dn_after.get((1 - half, 14 + dc), 0))
                P.wait("sync", "dnmm", prev)
                if half == 0 and dc == 0:
                    P.wait("sync", "cc2", 3)
                P.dma("sync", ds[:].rearrange("p (c i) -> p c i", c=64),
                      cc_td_out[:, dc * 128:(dc + 1) * 128].rearrange(
                          "(c p) i -> p c i", p=128), sem=dsem)
                P.wait("tensor", dsem, P.c(dsem))
                if dc >= 2:
                    P.wait("tensor", "dne", dne_after[(half, dc - 2)])
                elif half == 1:
                    P.wait("tensor", "dne", dne_after[(0, 14 + dc)])
                for ic in range(64):
                    T(lambda e, d=ds, i=ic, dcc=dc: e.matmul(
                        ps_b[:, (dcc % 2) * 512:(dcc % 2) * 512 + 512],
                        d[:, i * 128:(i + 1) * 128],
                        zq_T[:, i * TW:(i + 1) * TW],
                        start=(i == 0), stop=(i == 63)), sem="dnmm", n=1)
                dn_after[(half, dc)] = P.c("dnmm")
                P.wait("vector", "dnmm", P.c("dnmm"))
                P.wait("vector", "dtp", P.c("dtp"))
                P.wait("vector", "dst", P.c("dst"))
                V(lambda e, dcc=dc: e.scalar_tensor_tensor(
                    ffd_e[:, :], ps_b[:, (dcc % 2) * 512:(dcc % 2) * 512 + 512], 1.0,
                    gbc[:, goff + 512:goff + 1024], OP.mult, OP.mult), sem="dne", n=1)
                dne_after[(half, dc)] = P.c("dne")
                P.wait("tensor", "dne", P.c("dne"))
                P.wait("tensor", "dte", P.c("dte"))
                for t4 in range(4):
                    T(lambda e, t=t4: e.transpose(
                        ps_c[:, t * 128:t * 128 + 128],
                        ffd_e[:, t * 128:(t + 1) * 128], c_identf[:]), sem="dtp", n=1)
                P.wait("scalar", "dtp", P.c("dtp"))
                P.wait("scalar", "dst", P.c("dst"))
                P.wait("scalar", "dne", P.c("dne"))
                S(lambda e: e.copy(ffd_e[:, 0:512], ps_c[:, 0:512]), sem="dte", n=1)
                P.wait("gpsimd", "dte", P.c("dte"))
                for t4 in range(4):
                    P.dma("gpsimd",
                          ffn_park[half * 512 + t4 * 128:half * 512 + t4 * 128 + 128,
                                   dc * 128:(dc + 1) * 128],
                          ffd_e[:, t4 * 128:(t4 + 1) * 128], sem="dst")
                if fill and dc >= 4 and dc % 2 == 0 and fidx[0] < len(fill):
                    fill[fidx[0]]()
                    fidx[0] += 1
            while fidx[0] < len(fill):
                fill[fidx[0]]()
                fidx[0] += 1

        for st in mk_stats_steps(0):
            st()
        sq_quant(0)
        load_y1()
        down(0, fill=mk_stats_steps(1))
        sq_quant(1)
        down(1)
        if "ffn" in taps:
            P.wait("gpsimd", "dst", P.c("dst"))
            P.dma("gpsimd", taps["ffn"], ffn_park[:], sem="tapd")
        P.barrier("bar_ffn")
    ph3y.close()


    with ExitStack() as ph4:
        sb4 = lambda nm, shp, dt=F32: ph4.enter_context(nc.sbuf_tensor(nm, shp, dt))
        fin_x = sb4("fin_x", [128, D])
        fin_x2 = sb4("fin_x2", [128, D])
        fin_r = sb4("fin_r", [128, D], BF16)
        fin_r2 = sb4("fin_r2", [128, D], BF16)
        fin_f = sb4("fin_f", [128, D])
        fin_f2 = sb4("fin_f2", [128, D])
        fin_o = sb4("fin_o", [128, 4 * D])
        fin_o2 = None

        P.wait("sync", "cc3", P.c("cc3"))
        P.wait("sync", "dst", P.c("dst"))
        fo_after = {}
        for tt in range(8):
            par5 = tt % 2
            fx_t = fin_x[:, 0:D] if par5 == 0 else fin_x2[:, 0:D]
            fr_t = fin_r[:, 0:D] if par5 == 0 else fin_r2[:, 0:D]
            ff_t = fin_f[:, 0:D] if par5 == 0 else fin_f2[:, 0:D]
            fo_t = fin_o[:, (tt % 4) * D:(tt % 4) * D + D]
            if tt >= 2:
                P.wait("sync", "fo", fo_after[tt - 2])
            P.dma("sync", fx_t, x_own[tt * 128:(tt + 1) * 128, :], sem="fx")
            rsrc = rs_outA if tt < 4 else rs_outB
            roff = (tt % 4) * 128
            P.dma("sync", fr_t, rsrc[roff:roff + 128, :], sem="fx")
            P.dma("sync", ff_t, ffn_park[tt * 128:(tt + 1) * 128, :], sem="fx")
            P.wait("vector", "fx", P.c("fx"))
            if tt >= 4:
                P.wait("vector", "fst", 16 * (tt - 3))
            V(lambda e, a=fx_t, b=fr_t, o=fo_t: e.tensor_tensor(o, a, b, OP.add),
              sem="fo", n=1)
            P.wait_cur("vector", "fo")
            V(lambda e, o=fo_t, f=ff_t: e.tensor_tensor(o, o, f, OP.add),
              sem="fo", n=1)
            fo_after[tt] = P.c("fo")
            P.wait("gpsimd", "fo", P.c("fo"))
            P.dma("gpsimd", out_own[tt * 128:(tt + 1) * 128, :], fo_t, sem="fst")
        P.wait("gpsimd", "fst", P.c("fst"))
        if taps:
            P.wait("gpsimd", "tapd", P.c("tapd"))
        for s in sorted(P.cnt.keys()):
            P.wait("gpsimd", s, P.cnt[s])

        P.replay(blk)

    ctx.close()
    return nc



_consts = None


def _get_consts():
    global _consts
    if _consts is None:
        _consts = _host_consts()
    return _consts


def make_in_maps(L0_attn_x, L0_attn_Wq, L0_attn_Wk, L0_attn_Wv, L0_attn_Wo,
                 L0_ffn_w_gate, L0_ffn_w_up, L0_ffn_w_down):
    c = _get_consts()
    x = np.ascontiguousarray(np.asarray(L0_attn_x, np.float32).reshape(NTOK, D))
    wqT = np.asarray(L0_attn_Wq, np.float32).T
    wkT = np.asarray(L0_attn_Wk, np.float32).T
    wvT = np.asarray(L0_attn_Wv, np.float32).T
    woT = np.asarray(L0_attn_Wo, np.float32).T
    gT = np.asarray(L0_ffn_w_gate, np.float32).T
    uT = np.asarray(L0_ffn_w_up, np.float32).T
    dT = np.asarray(L0_ffn_w_down, np.float32).T
    in_maps = []
    for cc in range(NCORE):
        in_maps.append(dict(
            x_own=np.ascontiguousarray(x[cc * TOWN:(cc + 1) * TOWN, :]),
            wqT_s=np.ascontiguousarray(wqT[:, cc * FOWN:(cc + 1) * FOWN]),
            wkT_s=np.ascontiguousarray(wkT[:, cc * FOWN:(cc + 1) * FOWN]),
            wvT_s=np.ascontiguousarray(wvT[:, cc * FOWN:(cc + 1) * FOWN]),
            woT_s=np.ascontiguousarray(woT[cc * FOWN:(cc + 1) * FOWN, :]),
            gT_s=np.ascontiguousarray(gT[cc * FOWN:(cc + 1) * FOWN, :]),
            uT_s=np.ascontiguousarray(uT[cc * FOWN:(cc + 1) * FOWN, :]),
            dT_s=np.ascontiguousarray(dT[cc * TOWN:(cc + 1) * TOWN, :]),
            cosT=c["cosT"], sinT=c["sinT"], H128b=c["H128b"],
            identb=c["identb"], identf=c["identf"], onesf=c["onesf"],
            maskb=c["maskb"],
        ))
    return in_maps


def kernel(L0_attn_x, L0_attn_Wq, L0_attn_Wk, L0_attn_Wv, L0_attn_Wo,
           L0_ffn_w_gate, L0_ffn_w_up, L0_ffn_w_down):
    in_maps = make_in_maps(L0_attn_x, L0_attn_Wq, L0_attn_Wk, L0_attn_Wv,
                           L0_attn_Wo, L0_ffn_w_gate, L0_ffn_w_up, L0_ffn_w_down)
    nc = build_program()
    res = run_bass_kernel_spmd(nc, in_maps, core_ids=list(range(NCORE)))
    out = np.concatenate([np.asarray(r["out_own"], np.float32)
                          for r in res.results], axis=0)
    return out.reshape(1, NTOK, D)
